# version 04
# baseline (speedup 1.0000x reference)
import numpy as np
from contextlib import ExitStack

import concourse.bass as bass
import concourse.mybir as mybir
from concourse.bass_utils import run_bass_kernel_spmd

F32 = mybir.dt.float32
BF16 = mybir.dt.bfloat16
I32 = mybir.dt.int32
U8 = mybir.dt.uint8
AF = mybir.ActivationFunctionType
ALU = mybir.AluOpType
AX = mybir.AxisListType

P = 128
T = 1024
THL = 1026
D = 4096
ND = 32
W3 = 3072
NW3 = 24
NWG = 12
WA = 1024
NWA = 8
NH = 16
DH = 64
DFF = 11008
NF = 86
FQ = [(0, 22), (22, 22), (44, 21), (65, 21)]
EPS = 1e-5
ENGINES = ("sync", "gpsimd", "tensor", "vector", "scalar")
ENG2SEM = {"sync": "sy", "gpsimd": "gq", "tensor": "tn",
           "vector": "vc", "scalar": "ac"}


class Builder:
    def __init__(self, nc, ctx):
        self.nc = nc
        self.ctx = ctx
        self.sched = {e: [] for e in ENGINES}
        self.sems = {}
        for name in ("sy", "gq", "tn", "vc", "ac"):
            self.new_sem(name)
        self.hiwater = {}

    def new_sem(self, name):
        h = self.ctx.enter_context(self.nc.semaphore(name))
        self.sems[name] = [h, 0]
        return name

    def emit(self, eng, fn):
        self.sched[eng].append(fn)

    def val(self, s):
        return self.sems[s][1]

    def wait(self, eng, s, v):
        if v is None or v <= 0:
            return
        if v <= self.hiwater.get((eng, s), 0):
            return
        self.hiwater[(eng, s)] = v
        h = self.sems[s][0]
        self.emit(eng, lambda q, h=h, v=v: q.wait_ge(h, v))

    def run(self, eng, fn, inc=None, n=1, fence=False):
        if fence and inc is None:
            inc = ENG2SEM[eng]
        if inc is None:
            self.emit(eng, fn)
            return None
        rec = self.sems[inc]
        rec[1] += n
        h = rec[0]
        self.emit(eng, lambda q, fn=fn, h=h, n=n: fn(q).then_inc(h, n))
        if fence:
            self.wait(eng, inc, rec[1])
        return rec[1]

    def qwait(self, eng, s):
        self.wait(eng, s, self.val(s))

    def dma(self, eng, out_ap, in_ap, sem=None, **kw):
        s = sem if sem else ("sy" if eng == "sync" else "gq")
        rec = self.sems[s]
        rec[1] += 16
        h = rec[0]
        self.emit(eng, lambda q, o=out_ap, i=in_ap, h=h, kw=kw:
                  q.dma_start(out=o, in_=i, **kw).then_inc(h, 16))
        return rec[1]

    def barrier(self):
        for eng, s in ENG2SEM.items():
            self.run(eng, lambda q: q.drain(), inc=s)
        snap = {s: self.val(s) for s in self.sems}
        for eng in ENGINES:
            for s, v in snap.items():
                if s != ENG2SEM[eng]:
                    self.wait(eng, s, v)

    def finalize(self):
        with self.nc.Block() as block:
            for eng in ENGINES:
                def body(q, eng=eng):
                    for f in self.sched[eng]:
                        f(q)
                getattr(block, eng)(body)


def build_nc(debug=False):
    nc = bass.Bass("TRN2", target_bir_lowering=False)

    x_ext = nc.dram_tensor("x_ext", [THL, D], F32, kind="ExternalInput").ap()
    conv_t = nc.dram_tensor("conv_t", [NWG, 3, ND, P, 256], F32, kind="ExternalInput").ap()
    conv_b24 = nc.dram_tensor("conv_b24", [NW3, P], F32, kind="ExternalInput").ap()
    wq2 = nc.dram_tensor("wq2", [WA, WA], F32, kind="ExternalInput").ap()
    wk2 = nc.dram_tensor("wk2", [WA, WA], F32, kind="ExternalInput").ap()
    wv2 = nc.dram_tensor("wv2", [WA, WA], F32, kind="ExternalInput").ap()
    wo2 = nc.dram_tensor("wo2", [WA, WA], F32, kind="ExternalInput").ap()
    bq8 = nc.dram_tensor("bq8", [8, P], F32, kind="ExternalInput").ap()
    bk8 = nc.dram_tensor("bk8", [8, P], F32, kind="ExternalInput").ap()
    bv_row = nc.dram_tensor("bv_row", [1, WA], F32, kind="ExternalInput").ap()
    bo8 = nc.dram_tensor("bo8", [8, P], F32, kind="ExternalInput").ap()
    projw = nc.dram_tensor("projw", [WA, D], F32, kind="ExternalInput").ap()
    pb_row = nc.dram_tensor("pb_row", [1, D], F32, kind="ExternalInput").ap()
    wg_t = nc.dram_tensor("wg_t", [NF, 8, P, 512], I32, kind="ExternalInput").ap()
    wu_t = nc.dram_tensor("wu_t", [NF, 8, P, 512], I32, kind="ExternalInput").ap()
    wd_t = nc.dram_tensor("wd_t", [NF, 8, P, 512], I32, kind="ExternalInput").ap()
    gs86 = nc.dram_tensor("gs86", [NF, P], F32, kind="ExternalInput").ap()
    us86 = nc.dram_tensor("us86", [NF, P], F32, kind="ExternalInput").ap()
    ds_row_d = nc.dram_tensor("ds_row", [1, D], F32, kind="ExternalInput").ap()
    out_d = nc.dram_tensor("out", [T, D], F32, kind="ExternalOutput").ap()
    spill_kind = "ExternalOutput" if debug else "Internal"
    spill = nc.dram_tensor("qkv_spill", [NW3, P, T], BF16, kind=spill_kind).ap()
    if debug:
        dbg_xnt = nc.dram_tensor("dbg_xnt", [ND, P, THL], BF16, kind="ExternalOutput").ap()
        dbg_h0 = nc.dram_tensor("dbg_h0", [22, P, T], BF16, kind="ExternalOutput").ap()
        dbg_heads = nc.dram_tensor("dbg_heads", [32, P, T], BF16, kind="ExternalOutput").ap()
        dbg_ow = nc.dram_tensor("dbg_ow", [8, P, T], BF16, kind="ExternalOutput").ap()
        dbg_att = nc.dram_tensor("dbg_att", [8, P, P], F32, kind="ExternalOutput").ap()
        dbg_q = nc.dram_tensor("dbg_q", [4, T, D], F32, kind="ExternalOutput").ap()

    with ExitStack() as ctx:
        b = Builder(nc, ctx)

        PS = ctx.enter_context(nc.psum_tensor("ps", [P, 4096], F32))
        PSB = PS.bitcast(BF16)

        def bankf(i, n=1):
            return PS[:, i * 512:(i + n) * 512]

        def bankb(i, n=1):
            return PSB[:, i * 1024:(i + n) * 1024]

        AR_BYTES = 212000
        arena = ctx.enter_context(nc.sbuf_tensor("arena", [P, AR_BYTES], U8))
        A32h = arena.bitcast(F32)
        ABFh = arena.bitcast(BF16)
        A32 = A32h.ap()
        ABF = ABFh.ap()

        class Alloc:
            cur = 0

            def take(self, nbytes, align=64):
                self.cur = (self.cur + align - 1) // align * align
                off = self.cur
                self.cur += nbytes
                assert self.cur <= AR_BYTES, f"SBUF arena overflow: {self.cur}"
                return off
        al = Alloc()

        def f32v(off, n, p0=0, pn=P):
            return A32[p0:p0 + pn, off // 4: off // 4 + n]

        def bf16v(off, n, p0=0, pn=P):
            return ABF[p0:p0 + pn, off // 2: off // 2 + n]

        o_identb = al.take(256)
        o_identf = al.take(512)
        o_mask = al.take(512)
        o_ones = al.take(512)
        o_rows = al.take(512)
        o_cb = al.take(128)
        o_sbq = al.take(64)
        o_bk = al.take(64)
        o_bo = al.take(64)
        o_gs = al.take(384)
        o_us = al.take(384)
        o_dsb = al.take(16384)
        o_pbb = al.take(16384)
        o_bvb = al.take(4096)
        o_eps = al.take(64)
        o_bn = al.take(256)
        o_agg = al.take(64)
        o_lnt = al.take(64)
        o_rstd = al.take(64)
        o_negmb = al.take(64)
        REG_A = al.take(65664)
        REG_B = al.take(49152)
        REG_C = al.take(49152)
        assert al.cur <= AR_BYTES

        identb = bf16v(o_identb, 128)
        identf = f32v(o_identf, 128)
        mask = f32v(o_mask, 128)
        ones1 = f32v(o_ones, 128, 0, 1)

        def xnt(d_):
            return bf16v(REG_A + d_ * 2052, THL)

        def xnt3d(d0, nd_, c0, ncols):
            base = REG_A // 2
            return ABF[:, base:base + 32 * THL] \
                .rearrange("p (c n) -> p c n", n=THL)[:, d0:d0 + nd_, c0:c0 + ncols]

        def qkvT(w):
            return bf16v(REG_B + w * 2048, T)

        def hT(i):
            return bf16v(REG_B + i * 2048, T)

        def col(off, j):
            return A32[0:P, off // 4 + j: off // 4 + j + 1]

        def g_(fn, inc=None, n=1):
            return b.run("gpsimd", fn, inc=inc, n=n)

        g_(lambda q: q.memset(identb, 0.0))
        g_(lambda q: q.affine_select(out=identb, in_=identb, compare_op=ALU.not_equal,
                                     fill=1.0, base=0, pattern=[[-1, P]], channel_multiplier=1))
        g_(lambda q: q.memset(identf, 0.0))
        g_(lambda q: q.affine_select(out=identf, in_=identf, compare_op=ALU.not_equal,
                                     fill=1.0, base=0, pattern=[[-1, P]], channel_multiplier=1))
        g_(lambda q: q.memset(mask, 0.0))
        g_(lambda q: q.affine_select(out=mask, in_=mask, compare_op=ALU.is_ge,
                                     fill=-1e9, base=0, pattern=[[-1, P]], channel_multiplier=1))
        g_(lambda q: q.memset(ones1, 1.0))
        g_(lambda q: q.memset(f32v(o_eps, 1), EPS))
        setup_ms = g_(lambda q: q.drain(), inc="gq", n=1)

        rowbuf = lambda k: f32v(o_rows, 128, 0, k)

        def vec_to_cols(dram_ap, k, dest_off):
            t_ld = b.dma("sync", rowbuf(k), dram_ap)
            b.wait("tensor", "sy", t_ld)
            b.wait("tensor", "gq", setup_ms)
            ms = b.run("tensor", lambda q, k=k: q.transpose(
                PS[0:P, 0:k], rowbuf(k), identf[0:k, 0:k]), inc="tn")
            b.wait("vector", "tn", ms)
            b.run("vector", lambda q, k=k, d=dest_off: q.tensor_copy(
                A32[0:P, d // 4:d // 4 + k], PS[0:P, 0:k]), inc="vc")
            ev = b.run("vector", lambda q: q.drain(), inc="vc")
            b.wait("tensor", "vc", ev)
            b.wait("sync", "tn", ms)
            return ev

        vec_to_cols(conv_b24[:, :], NW3, o_cb)
        vec_to_cols(bq8[:, :], 8, o_sbq)
        vec_to_cols(bk8[:, :], 8, o_bk)
        vec_to_cols(bo8[:, :], 8, o_bo)
        vec_to_cols(gs86[:, :], NF, o_gs)
        ev_us = vec_to_cols(us86[:, :], NF, o_us)
        b.wait("scalar", "vc", ev_us)
        sbq_ms = b.run("scalar", lambda q: q.mul(
            A32[0:P, o_sbq // 4:o_sbq // 4 + 8],
            A32[0:P, o_sbq // 4:o_sbq // 4 + 8], 0.125), inc="ac")

        o_dsrow = REG_C
        o_pbrow = REG_C + 16384
        o_bvrow = REG_C + 32768
        b.new_sem("rw")
        t1 = b.dma("sync", f32v(o_dsrow, D, 0, 1), ds_row_d, sem="rw")
        t2 = b.dma("sync", f32v(o_pbrow, D, 0, 1), pb_row, sem="rw")
        t3 = b.dma("sync", f32v(o_bvrow, WA, 0, 1), bv_row, sem="rw")
        b.wait("tensor", "rw", t3)
        for src_off, dst_off, n in ((o_dsrow, o_dsb, D), (o_pbrow, o_pbb, D),
                                    (o_bvrow, o_bvb, WA)):
            for j in range(n // 512):
                ms = b.run("tensor", lambda q, s=src_off, j=j: q.matmul(
                    bankf(1), ones1, A32[0:1, s // 4 + j * 512: s // 4 + (j + 1) * 512],
                    start=True, stop=True), inc="tn")
                b.wait("vector", "tn", ms)
                b.run("vector", lambda q, d=dst_off, j=j: q.tensor_copy(
                    A32[0:P, d // 4 + j * 512: d // 4 + (j + 1) * 512], bankf(1)), inc="vc")
                ev = b.run("vector", lambda q: q.drain(), inc="vc")
                b.wait("tensor", "vc", ev)

        b.barrier()

        o_xring = REG_B
        o_xnring = REG_B + 32768
        bn = f32v(o_bn, 48)
        agg = f32v(o_agg, 2)
        b.new_sem("xr0"); b.new_sem("xr1")
        xn_ms = {}
        trp_ms = {}
        ln_ld = {}
        ev_ms = {}
        for t_ in range(9):
            rt = 128 if t_ < 8 else 2
            xv = f32v(o_xring + (t_ % 2) * 16384, D, 0, rt)
            if t_ >= 2:
                b.wait("sync", "vc", xn_ms[t_ - 2])
            ln_ld[t_] = b.dma("sync", xv, x_ext[t_ * P: t_ * P + rt, :],
                              sem=f"xr{t_ % 2}")
            b.wait("vector", f"xr{t_ % 2}", ln_ld[t_])
            for i in range(8):
                b.run("vector", lambda q, xv=xv, i=i, rt=rt: q.bn_stats(
                    bn[0:rt, 6 * i:6 * i + 6], xv[0:rt, 512 * i:512 * (i + 1)]),
                    inc="vc" if i == 7 else None, fence=(i == 7))
            st_ms = b.run("vector", lambda q, rt=rt: q.bn_aggr(
                agg[0:rt, 0:2], bn[0:rt, 0:48]), inc="vc")
            b.wait("scalar", "vc", st_ms)
            b.run("scalar", lambda q, rt=rt: q.activation(
                f32v(o_lnt, 1, 0, rt), agg[0:rt, 1:2], AF.Ln,
                bias=f32v(o_eps, 1, 0, rt), scale=1.0), fence=True)
            rs_ms = b.run("scalar", lambda q, rt=rt: q.activation(
                f32v(o_rstd, 1, 0, rt), f32v(o_lnt, 1, 0, rt), AF.Exp,
                bias=0.0, scale=-0.5), inc="ac")
            b.wait("vector", "ac", rs_ms)
            nm_ms = b.run("vector", lambda q, rt=rt: q.scalar_tensor_tensor(
                f32v(o_negmb, 1, 0, rt), agg[0:rt, 0:1], -1.0,
                f32v(o_rstd, 1, 0, rt), op0=ALU.mult, op1=ALU.mult),
                inc="vc", fence=True)
            if t_ >= 2:
                b.wait("vector", "tn", trp_ms[t_ - 2])
            xnv = bf16v(o_xnring + (t_ % 2) * 8192, D, 0, rt)
            b.run("vector", lambda q, xv=xv, xnv=xnv, rt=rt: q.tensor_scalar(
                xnv, xv, f32v(o_rstd, 1, 0, rt), f32v(o_negmb, 1, 0, rt),
                op0=ALU.mult, op1=ALU.add), inc="vc", fence=True)
            b.run("vector", lambda q, xv=xv, rt=rt: q.tensor_tensor(
                xv, xv, f32v(o_pbb, D, 0, rt), op=ALU.add), inc="vc", fence=True)
            xn_ms[t_] = b.run("vector", lambda q: q.drain(), inc="vc")
            b.wait("sync", "vc", xn_ms[t_])
            if t_ == 0:
                b.dma("sync", out_d[0:127, :], f32v(o_xring + 0, D, 1, 127))
            elif t_ < 8:
                b.dma("sync", out_d[t_ * P - 1: (t_ + 1) * P - 1, :], xv)
            else:
                b.dma("sync", out_d[T - 1: T, :], f32v(o_xring + (t_ % 2) * 16384, D, 0, 1))
            b.wait("tensor", "vc", xn_ms[t_])
            for g in range(8):
                gg = t_ * 8 + g
                bk_ = 2 + gg % 2
                if gg >= 2:
                    b.wait("tensor", "vc", ev_ms[gg - 2])
                for j in range(4):
                    d_ = g * 4 + j
                    last = (j == 3)
                    m = b.run("tensor", lambda q, bk_=bk_, j=j, d_=d_, xnv=xnv, rt=rt: q.transpose(
                        bankb(bk_)[:, j * 128:j * 128 + rt],
                        xnv[0:rt, d_ * 128:(d_ + 1) * 128], identb[0:rt, 0:rt]),
                        inc="tn" if last else None)
                    if last:
                        grp_ms = m
                if g == 7:
                    trp_ms[t_] = grp_ms
                b.wait("vector", "tn", grp_ms)
                src3 = bankb(bk_).rearrange("p (c n) -> p c n", n=128)[:, 0:4, 0:rt]
                ev_ms[gg] = b.run("vector", lambda q, g=g, t_=t_, rt=rt, src3=src3:
                                  q.tensor_copy(xnt3d(g * 4, 4, t_ * 128, rt), src3),
                                  inc="vc")
        b.barrier()
        if debug:
            for d_ in range(ND):
                b.dma("sync", dbg_xnt[d_, :, :], xnt(d_))
            b.barrier()

        o_cring = REG_C
        CR = 12
        for s_ in range(CR):
            b.new_sem(f"cr{s_}")
        cons = []
        evac_ms = {}
        for wg in range(NWG):
            base = 0 if wg % 2 == 0 else 4
            if wg >= 2:
                b.wait("tensor", "ac", evac_ms[wg - 2])
            for tap in range(3):
                for d_ in range(ND):
                    gi = wg * 96 + tap * 32 + d_
                    slot = gi % CR
                    if gi >= CR:
                        b.wait("gpsimd", "tn", cons[gi - CR])
                    ld = b.dma("gpsimd", bf16v(o_cring + slot * 512, 256),
                               conv_t[wg, tap, d_, :, :], sem=f"cr{slot}")
                    b.wait("tensor", f"cr{slot}", ld)
                    first = (tap == 0 and d_ == 0)
                    lastk = (tap == 2 and d_ == ND - 1)
                    for wt in range(2):
                        for c in range(2):
                            bk_ = base + wt * 2 + c
                            is_last_of_tile = (wt == 1 and c == 1)
                            b.run("tensor", lambda q, bk_=bk_, slot=slot, wt=wt, c=c,
                                  d_=d_, tap=tap, first=first, lastk=lastk: q.matmul(
                                bankf(bk_),
                                bf16v(o_cring + slot * 512, 256)[:, wt * 128:(wt + 1) * 128],
                                xnt(d_)[:, c * 512 + tap: c * 512 + tap + 512],
                                start=first, stop=lastk),
                                inc="tn" if is_last_of_tile else None)
                    cons.append(b.val("tn"))
            stop_all = cons[-1]
            b.wait("scalar", "tn", stop_all)
            for wt in range(2):
                for c in range(2):
                    w_ = wg * 2 + wt
                    e = b.run("scalar", lambda q, base=base, wt=wt, c=c, w_=w_: q.activation(
                        qkvT(w_)[:, c * 512:(c + 1) * 512], bankf(base + wt * 2 + c),
                        AF.Identity, bias=col(o_cb, w_), scale=1.0), inc="ac")
            evac_ms[wg] = e
            e = b.run("scalar", lambda q: q.drain(), inc="ac")
            b.wait("sync", "ac", e)
            for wt in range(2):
                w_ = wg * 2 + wt
                b.dma("sync", spill[w_, :, :], qkvT(w_))
        b.barrier()

        o_sil = REG_C
        o_gring = REG_C + 8192
        o_uring = REG_C + 19456
        o_wdring = REG_C + 30720
        o_ftmp = REG_C + 36864
        o_pin = REG_C + 40960
        GR = 11
        WR = 6
        for s_ in range(GR):
            b.new_sem(f"gr{s_}")
            b.new_sem(f"ur{s_}")
        for s_ in range(WR):
            b.new_sem(f"wr{s_}")
        gslotv = lambda s: bf16v(o_gring + s * 1024, 512)
        uslotv = lambda s: bf16v(o_uring + s * 1024, 512)
        wdslotv = lambda s: bf16v(o_wdring + s * 1024, 512)
        ftmpv = lambda s: f32v(o_ftmp + s * 2048, 512)
        pinv = lambda s: f32v(o_pin + s * 2048, 512)

        for s_ in range(4):
            b.new_sem(f"pn{s_}")
        gu_cons = []
        gu_loads = []
        flush_ms = {}
        for qf, (f0, nf_) in enumerate(FQ):
            sil_ms = {}
            h_ms = {}
            for fi in range(nf_):
                ft = f0 + fi
                par = fi % 2
                pgate = PS[:, (par * 2048):(par * 2048 + 1024)]
                pup = PS[:, (par * 2048 + 1024):(par * 2048 + 2048)]
                if fi >= 2:
                    b.wait("tensor", "ac", sil_ms[fi - 2])
                    b.wait("tensor", "vc", h_ms[fi - 2])
                ldg = [None] * 8
                ldu = [None] * 8
                for dg in range(8):
                    gi = len(gu_loads)
                    if gi >= 2 * GR:
                        b.wait("gpsimd", "tn", gu_cons[gi - 2 * GR])
                    sl = (fi * 8 + dg) % GR
                    ldg[dg] = (sl, b.dma("gpsimd", gslotv(sl), wg_t[ft, dg, :, :],
                                          sem=f"gr{sl}"))
                    gu_loads.append(ldg[dg]); gu_cons.append(None)
                    gi = len(gu_loads)
                    if gi >= 2 * GR:
                        b.wait("gpsimd", "tn", gu_cons[gi - 2 * GR])
                    ldu[dg] = (sl, b.dma("gpsimd", uslotv(sl), wu_t[ft, dg, :, :],
                                          sem=f"ur{sl}"))
                    gu_loads.append(ldu[dg]); gu_cons.append(None)
                for d_ in range(ND):
                    dg, j = d_ // 4, d_ % 4
                    sl = (fi * 8 + dg) % GR
                    b.wait("tensor", f"gr{sl}", ldg[dg][1])
                    b.wait("tensor", f"ur{sl}", ldu[dg][1])
                    first = (d_ == 0)
                    lastk = (d_ == ND - 1)
                    for mat, psum, ring in ((0, pgate, gslotv(sl)), (1, pup, uslotv(sl))):
                        for c in range(2):
                            do_inc = (mat == 1 and c == 1 and j == 3)
                            b.run("tensor", lambda q, psum=psum, ring=ring, j=j, c=c,
                                  d_=d_, first=first, lastk=lastk: q.matmul(
                                psum[:, c * 512:(c + 1) * 512],
                                ring[:, j * 128:(j + 1) * 128],
                                xnt(d_)[:, 1 + c * 512: 1 + c * 512 + 512],
                                start=first, stop=lastk),
                                inc="tn" if do_inc else None)
                    if j == 3:
                        v = b.val("tn")
                        gu_cons[len(gu_loads) - 16 + dg * 2] = v
                        gu_cons[len(gu_loads) - 16 + dg * 2 + 1] = v
                stop_ms = b.val("tn")
                b.wait("scalar", "tn", stop_ms)
                if fi >= 2:
                    b.wait("scalar", "vc", h_ms[fi - 2])
                sil_ms[fi] = b.run("scalar", lambda q, par=par, pgate=pgate, ft=ft: q.activation(
                    f32v(o_sil + par * 4096, 1024), pgate, AF.Silu,
                    bias=0.0, scale=col(o_gs, ft)), inc="ac")
                b.wait("vector", "ac", sil_ms[fi])
                b.wait("vector", "tn", stop_ms)
                h_ms[fi] = b.run("vector", lambda q, fi=fi, par=par, pup=pup, ft=ft: q.scalar_tensor_tensor(
                    hT(fi), pup, col(o_us, ft), f32v(o_sil + par * 4096, 1024),
                    op0=ALU.mult, op1=ALU.mult), inc="vc")
            last_sil = sil_ms[nf_ - 1]
            last_h = h_ms[nf_ - 1]
            wd_cons = []
            wd_loads = []
            b.wait("tensor", "ac", last_sil)
            b.wait("tensor", "vc", last_h)
            for dt in range(8):
                bank_stop = [None] * 8
                pin_ld = {ts: b.dma("sync", pinv(ts % 4), out_d[
                    ts * P:(ts + 1) * P, dt * 512:(dt + 1) * 512],
                    sem=f"pn{ts % 4}")
                    for ts in range(4)}
                for fi in range(nf_):
                    ft = f0 + fi
                    gi = len(wd_loads)
                    if gi >= WR:
                        b.wait("gpsimd", "tn", wd_cons[gi - WR])
                    sl = gi % WR
                    ld = b.dma("gpsimd", wdslotv(sl), wd_t[ft, dt, :, :],
                               sem=f"wr{sl}")
                    wd_loads.append(ld)
                    b.wait("tensor", f"wr{sl}", ld)
                    for ts in range(8):
                        if fi == 0 and dt > 0:
                            b.wait("tensor", "vc", flush_ms[(dt - 1, ts)])
                        do_inc = (ts == 7) or (fi == nf_ - 1)
                        m = b.run("tensor", lambda q, ts=ts, fi=fi, sl=sl, nf_=nf_: q.matmul(
                            bankf(ts), hT(fi)[:, ts * 128:(ts + 1) * 128], wdslotv(sl),
                            start=(fi == 0), stop=(fi == nf_ - 1)),
                            inc="tn" if do_inc else None)
                        if fi == nf_ - 1:
                            bank_stop[ts] = b.val("tn")
                    wd_cons.append(b.val("tn"))
                for ts in range(8):
                    sl = ts % 4
                    if ts >= 4:
                        pin_ld[ts] = b.dma("sync", pinv(sl), out_d[
                            ts * P:(ts + 1) * P, dt * 512:(dt + 1) * 512],
                            sem=f"pn{sl}")
                    b.wait("vector", "tn", bank_stop[ts])
                    b.wait("vector", f"pn{sl}", pin_ld[ts])
                    b.run("vector", lambda q, ts=ts, dt=dt: q.tensor_tensor(
                        ftmpv(ts % 2), bankf(ts),
                        A32[0:P, o_dsb // 4 + dt * 512: o_dsb // 4 + (dt + 1) * 512],
                        op=ALU.mult), inc="vc", fence=True)
                    fm = b.run("vector", lambda q, ts=ts, sl=sl: q.tensor_tensor(
                        pinv(sl), ftmpv(ts % 2), pinv(sl), op=ALU.add), inc="vc")
                    flush_ms[(dt, ts)] = fm
                    b.wait("sync", "vc", fm)
                    st = b.dma("sync", out_d[ts * P:(ts + 1) * P, dt * 512:(dt + 1) * 512],
                               pinv(sl))
                    if debug:
                        b.dma("sync", dbg_q[qf, ts * P:(ts + 1) * P,
                                            dt * 512:(dt + 1) * 512], pinv(sl))
            if debug and qf == 0:
                b.wait("sync", "vc", last_h)
                for fi in range(nf_):
                    b.dma("sync", dbg_h0[fi, :, :], hT(fi))
            b.barrier()
            flush_ms = {}

        qhT = lambda ht: bf16v(REG_A + ht * 2048, T)
        khT = lambda ht: bf16v(REG_A + 16384 + ht * 2048, T)
        vh = lambda ts: bf16v(REG_A + 32768 + ts * 2048, T)
        oT = lambda ht: bf16v(REG_A + 49152 + ht * 2048, T)
        owT = lambda wt: bf16v(REG_C + wt * 2048, T)
        hw = lambda wc: bf16v(REG_C + 16384 + wc * 2048, T)
        o_pring = REG_C + 32768
        o_pin3 = REG_C + 35840
        o_sm = REG_C + 41984
        o_pf = o_sm + 2048
        o_pbf = o_pf + 2048
        o_ptb = o_pbf + 1024
        o_nm = o_ptb + 512
        o_ssum = o_nm + 128
        o_rcp = o_ssum + 128
        pring = lambda s: bf16v(o_pring + s * 1024, 512)
        pin3 = lambda s: f32v(o_pin3 + s * 2048, 512)
        for s_ in range(3):
            b.new_sem(f"pr{s_}")
            b.new_sem(f"pq{s_}")
        for s_ in range(NWA):
            b.new_sem(f"hw{s_}")
        b.new_sem("qk")

        for w in range(NW3):
            b.dma("sync", qkvT(w), spill[w, :, :], sem="qk")
        qk_all = b.val("qk")

        def head_proj(wsrc, kind):
            hld = [b.dma("gpsimd", hw(wc), wsrc[wc * P:(wc + 1) * P, :],
                         sem=f"hw{wc}") for wc in range(NWA)]
            ev = {}
            if kind == "v":
                for ts in range(8):
                    par = ts % 2
                    ps = PS[:, par * 2048: par * 2048 + 1024]
                    if ts >= 2:
                        b.wait("tensor", "vc", ev[ts - 2])
                    for wc in range(NWA):
                        b.wait("tensor", f"hw{wc}", hld[wc])
                        b.wait("tensor", "qk", qk_all)
                        for c in range(2):
                            do_inc = (wc == NWA - 1 and c == 1)
                            b.run("tensor", lambda q, ps=ps, wc=wc, c=c, ts=ts: q.matmul(
                                ps[:, c * 512:(c + 1) * 512],
                                qkvT(16 + wc)[:, ts * 128:(ts + 1) * 128],
                                hw(wc)[:, c * 512:(c + 1) * 512],
                                start=(wc == 0), stop=(wc == NWA - 1)),
                                inc="tn" if do_inc else None)
                    sm_ = b.val("tn")
                    b.wait("vector", "tn", sm_)
                    ev[ts] = b.run("vector", lambda q, ts=ts, ps=ps: q.tensor_tensor(
                        vh(ts), ps, A32[0:P, o_bvb // 4: o_bvb // 4 + 1024],
                        op=ALU.add), inc="vc")
                return ev[7]
            else:
                src0 = 0 if kind == "q" else 8
                dest = qhT if kind == "q" else khT
                bias_off = o_sbq if kind == "q" else o_bk
                scale = 0.125 if kind == "q" else 1.0
                for ht in range(NWA):
                    par = ht % 2
                    ps = PS[:, par * 2048: par * 2048 + 1024]
                    if ht >= 2:
                        b.wait("tensor", "ac", ev[ht - 2])
                    for wc in range(NWA):
                        b.wait("tensor", f"hw{wc}", hld[wc])
                        b.wait("tensor", "qk", qk_all)
                        for c in range(2):
                            do_inc = (wc == NWA - 1 and c == 1)
                            b.run("tensor", lambda q, ps=ps, wc=wc, c=c, ht=ht, src0=src0: q.matmul(
                                ps[:, c * 512:(c + 1) * 512],
                                hw(wc)[:, ht * 128:(ht + 1) * 128],
                                qkvT(src0 + wc)[:, c * 512:(c + 1) * 512],
                                start=(wc == 0), stop=(wc == NWA - 1)),
                                inc="tn" if do_inc else None)
                    sm_ = b.val("tn")
                    b.wait("scalar", "tn", sm_)
                    ev[ht] = b.run("scalar", lambda q, ht=ht, ps=ps, dest=dest,
                                   bias_off=bias_off, scale=scale: q.activation(
                        dest(ht), ps, AF.Identity, bias=col(bias_off, ht), scale=scale),
                        inc="ac")
                return ev[7]

        head_proj(wq2, "q")
        b.barrier()
        head_proj(wk2, "k")
        b.barrier()
        head_proj(wv2, "v")
        b.barrier()
        if debug:
            for j in range(8):
                b.dma("sync", dbg_heads[j, :, :], qhT(j))
                b.dma("sync", dbg_heads[8 + j, :, :], khT(j))
                b.dma("sync", dbg_heads[16 + j, :, :], vh(j))
            b.barrier()

        sm_ms = {}
        exp_ms = {}
        rcp_ms = {}
        pbf_ms = {}
        pt_ms = {}
        ptev_ms = {}
        pv_ms = {}
        oev_ms = {}
        for blk in range(8):
            for h in range(NH):
                i = blk * NH + h
                par = i % 4
                ip = i // 2
                par2 = ip % 2
                ht = h // 2
                hrow = (h % 2) * 64
                ps_s = PS[:, par * 512: par * 512 + 128]
                if i >= 4:
                    b.wait("tensor", "vc", sm_ms[i - 4])
                b.run("tensor", lambda q, ps_s=ps_s, ht=ht, hrow=hrow, blk=blk: q.matmul(
                    ps_s,
                    qhT(ht)[hrow:hrow + 64, blk * 128:(blk + 1) * 128],
                    khT(ht)[hrow:hrow + 64, blk * 128:(blk + 1) * 128],
                    start=True, stop=True), inc="tn")
                s_ms = b.val("tn")
                b.wait("vector", "tn", s_ms)
                if i >= 4:
                    b.wait("vector", "ac", exp_ms[i - 4])
                smv = f32v(o_sm + par * 512, 128)
                b.run("vector", lambda q, smv=smv, ps_s=ps_s: q.tensor_tensor(
                    smv, ps_s, mask, op=ALU.add), inc="vc", fence=True)
                sm_ms[i] = b.run("vector", lambda q, smv=smv, par=par: q.reduce_max(
                    f32v(o_nm + par * 32, 1), smv, axis=AX.X, negate=True), inc="vc")
                b.wait("scalar", "vc", sm_ms[i])
                if i >= 4:
                    b.wait("scalar", "vc", pbf_ms[i - 4])
                pfv = f32v(o_pf + par * 512, 128)
                exp_ms[i] = b.run("scalar", lambda q, pfv=pfv, smv=smv, par=par: q.activation(
                    pfv, smv, AF.Exp, bias=f32v(o_nm + par * 32, 1), scale=1.0), inc="ac")
                b.wait("vector", "ac", exp_ms[i])
                if i >= 4:
                    b.wait("vector", "ac", rcp_ms[i - 4])
                sum_ms = b.run("vector", lambda q, pfv=pfv, par=par: q.reduce_sum(
                    f32v(o_ssum + par * 32, 1), pfv, axis=AX.X), inc="vc")
                b.wait("scalar", "vc", sum_ms)
                b.run("scalar", lambda q, par=par: q.activation(
                    f32v(o_rcp + par * 32, 1), f32v(o_ssum + par * 32, 1),
                    AF.Ln, bias=0.0, scale=1.0), fence=True)
                rcp_ms[i] = b.run("scalar", lambda q, par=par: q.activation(
                    f32v(o_rcp + par * 32, 1), f32v(o_rcp + par * 32, 1),
                    AF.Exp, bias=0.0, scale=-1.0), inc="ac")
                b.wait("vector", "ac", rcp_ms[i])
                pbfv = bf16v(o_pbf + par * 256, 128)
                if i >= 4:
                    b.wait("vector", "tn", pt_ms[i - 4])
                pbf_ms[i] = b.run("vector", lambda q, pbfv=pbfv, pfv=pfv, par=par: q.tensor_scalar(
                    pbfv, pfv, f32v(o_rcp + par * 32, 1), None, op0=ALU.mult), inc="vc")
                b.wait("tensor", "vc", pbf_ms[i])
                if i >= 2:
                    b.wait("tensor", "vc", ptev_ms[i - 2])
                b.run("tensor", lambda q, i=i, pbfv=pbfv: q.transpose(
                    bankb(4 + i % 2)[:, 0:128], pbfv, identb), inc="tn")
                pt_ms[i] = b.val("tn")
                b.wait("vector", "tn", pt_ms[i])
                ptbv = bf16v(o_ptb + (i % 2) * 256, 128)
                if i >= 2:
                    b.wait("vector", "tn", pv_ms[i - 2])
                ptev_ms[i] = b.run("vector", lambda q, ptbv=ptbv, i=i: q.tensor_copy(
                    ptbv, bankb(4 + i % 2)[:, 0:128]), inc="vc")
                if debug and i < 2:
                    dsl = i * 4
                    b.wait("vector", "ac", exp_ms[i])
                    c1 = b.run("vector", lambda q, dsl=dsl, ps_s=ps_s: q.tensor_copy(
                        f32v(o_sil + 2048 * (dsl % 2), 128), ps_s), inc="vc")
                    sm_ms[i] = c1
                    b.wait("sync", "vc", c1)
                    b.dma("sync", dbg_att[dsl, :, :], f32v(o_sil + 2048 * (dsl % 2), 128))
                    b.dma("sync", dbg_att[dsl + 1, :, :], smv)
                    b.dma("sync", dbg_att[dsl + 2, :, :], pfv)
                    b.wait("sync", "tn", pt_ms[i])
                    c2 = b.run("vector", lambda q, dsl=dsl, i=i: q.tensor_copy(
                        f32v(o_sil + 2048 * (dsl % 2), 128), bankb(4 + i % 2)[:, 0:128]), inc="vc")
                    ptev_ms[i] = c2
                    b.wait("sync", "vc", c2)
                    b.dma("sync", dbg_att[dsl + 3, :, :], f32v(o_sil + 2048 * (dsl % 2), 128))
                b.wait("tensor", "vc", ptev_ms[i])
                if h % 2 == 0 and ip >= 2:
                    b.wait("tensor", "vc", oev_ms[ip - 2])
                b.run("tensor", lambda q, hrow=hrow, par2=par2, blk=blk, h=h, ptbv=ptbv: q.matmul(
                    PS[hrow:hrow + 64, (6 + par2) * 512: (6 + par2) * 512 + 128],
                    vh(blk)[:, h * 64:(h + 1) * 64], ptbv,
                    start=True, stop=True), inc="tn")
                pv_ms[i] = b.val("tn")
                if h % 2 == 1:
                    op_ms = pv_ms[i]
                    b.wait("vector", "tn", op_ms)
                    oev_ms[ip] = b.run("vector", lambda q, ht=ht, blk=blk, par2=par2: q.tensor_copy(
                        oT(ht)[:, blk * 128:(blk + 1) * 128],
                        PS[:, (6 + par2) * 512: (6 + par2) * 512 + 128]), inc="vc")
        b.barrier()
        if debug:
            for j in range(8):
                b.dma("sync", dbg_heads[24 + j, :, :], oT(j))
            b.barrier()

        hld = [b.dma("gpsimd", hw(wc), wo2[wc * P:(wc + 1) * P, :],
                     sem=f"hw{wc}") for wc in range(NWA)]
        evo = {}
        for wt in range(NWA):
            par = wt % 2
            ps = PS[:, par * 2048: par * 2048 + 1024]
            if wt >= 2:
                b.wait("tensor", "ac", evo[wt - 2])
            for wc in range(NWA):
                b.wait("tensor", f"hw{wc}", hld[wc])
                for c in range(2):
                    do_inc = (wc == NWA - 1 and c == 1)
                    b.run("tensor", lambda q, ps=ps, wc=wc, c=c, wt=wt: q.matmul(
                        ps[:, c * 512:(c + 1) * 512],
                        hw(wc)[:, wt * 128:(wt + 1) * 128],
                        oT(wc)[:, c * 512:(c + 1) * 512],
                        start=(wc == 0), stop=(wc == NWA - 1)),
                        inc="tn" if do_inc else None)
            sm_ = b.val("tn")
            b.wait("scalar", "tn", sm_)
            evo[wt] = b.run("scalar", lambda q, wt=wt, ps=ps: q.activation(
                owT(wt), ps, AF.Identity, bias=col(o_bo, wt), scale=1.0), inc="ac")
        b.barrier()
        if debug:
            for j in range(8):
                b.dma("sync", dbg_ow[j, :, :], owT(j))
            b.barrier()

        pw_cons = []
        pw_loads = []
        fin_ms = {}
        ptmp_st = {}
        for dt in range(8):
            bank_stop = [None] * 8
            pin_ld = {ts: b.dma("sync", pin3(ts % 3), out_d[
                ts * P:(ts + 1) * P, dt * 512:(dt + 1) * 512],
                sem=f"pq{ts % 3}")
                for ts in range(3)}
            for wc in range(NWA):
                gi = len(pw_loads)
                if gi >= 3:
                    b.wait("gpsimd", "tn", pw_cons[gi - 3])
                sl = gi % 3
                ld = b.dma("gpsimd", pring(sl), projw[wc * P:(wc + 1) * P, dt * 512:(dt + 1) * 512],
                           sem=f"pr{sl}")
                pw_loads.append(ld)
                b.wait("tensor", f"pr{sl}", ld)
                for ts in range(8):
                    if wc == 0 and dt > 0:
                        b.wait("tensor", "vc", fin_ms[(dt - 1, ts)])
                    do_inc = (ts == 7) or (wc == NWA - 1)
                    b.run("tensor", lambda q, ts=ts, wc=wc, sl=sl: q.matmul(
                        bankf(ts), owT(wc)[:, ts * 128:(ts + 1) * 128], pring(sl),
                        start=(wc == 0), stop=(wc == NWA - 1)),
                        inc="tn" if do_inc else None)
                    if wc == NWA - 1:
                        bank_stop[ts] = b.val("tn")
                pw_cons.append(b.val("tn"))
            for ts in range(8):
                sl = ts % 3
                if ts >= 3:
                    pin_ld[ts] = b.dma("sync", pin3(sl), out_d[
                        ts * P:(ts + 1) * P, dt * 512:(dt + 1) * 512],
                        sem=f"pq{sl}")
                b.wait("vector", "tn", bank_stop[ts])
                b.wait("vector", f"pq{sl}", pin_ld[ts])
                fin_ms[(dt, ts)] = b.run("vector", lambda q, ts=ts, sl=sl: q.tensor_tensor(
                    pin3(sl), bankf(ts), pin3(sl), op=ALU.add), inc="vc")
                b.wait("sync", "vc", fin_ms[(dt, ts)])
                b.dma("sync", out_d[ts * P:(ts + 1) * P, dt * 512:(dt + 1) * 512],
                      pin3(sl))
        b.wait("sync", "gq", b.val("gq"))
        b.wait("sync", "sy", b.val("sy"))

        b.finalize()
    return nc


def _prep_shared(inputs):
    a = {k: np.asarray(v) for k, v in inputs.items()}
    conv_w = a["L0_attn_conv_w"].astype(np.float32, copy=False)
    conv_t = np.ascontiguousarray(
        conv_w.reshape(3, ND, P, NWG, 256).transpose(3, 0, 1, 2, 4))
    gw = a["L0_ffn_gate_w"]
    uw = a["L0_ffn_up_w"]
    dw = a["L0_ffn_down_w"]
    wgT = np.ascontiguousarray(gw.T)
    wuT = np.ascontiguousarray(uw.T)
    wdT = np.ascontiguousarray(dw.T)
    wg_t = np.ascontiguousarray(
        wgT.reshape(ND, P, NF, P).transpose(2, 0, 1, 3)
           .reshape(NF, 8, 4, P, P).transpose(0, 1, 3, 2, 4).reshape(NF, 8, P, 512))
    wu_t = np.ascontiguousarray(
        wuT.reshape(ND, P, NF, P).transpose(2, 0, 1, 3)
           .reshape(NF, 8, 4, P, P).transpose(0, 1, 3, 2, 4).reshape(NF, 8, P, 512))
    wd_t = np.ascontiguousarray(
        wdT.reshape(NF, P, 8, 512).transpose(0, 2, 1, 3))
    shared = {
        "conv_t": conv_t,
        "conv_b24": np.ascontiguousarray(a["L0_attn_conv_b"].reshape(NW3, P).astype(np.float32, copy=False)),
        "wq2": np.ascontiguousarray(a["L0_attn_wq"].reshape(WA, WA).astype(np.float32, copy=False)),
        "wk2": np.ascontiguousarray(a["L0_attn_wk"].reshape(WA, WA).astype(np.float32, copy=False)),
        "wv2": np.ascontiguousarray(a["L0_attn_wv"].reshape(WA, WA).astype(np.float32, copy=False)),
        "wo2": np.ascontiguousarray(a["L0_attn_wo"].reshape(WA, WA).astype(np.float32, copy=False)),
        "bq8": np.ascontiguousarray(a["L0_attn_bq"].reshape(8, P).astype(np.float32, copy=False)),
        "bk8": np.ascontiguousarray(a["L0_attn_bk"].reshape(8, P).astype(np.float32, copy=False)),
        "bv_row": np.ascontiguousarray(a["L0_attn_bv"].reshape(1, WA).astype(np.float32, copy=False)),
        "bo8": np.ascontiguousarray(a["L0_attn_bo"].reshape(8, P).astype(np.float32, copy=False)),
        "projw": np.ascontiguousarray(a["L0_attn_proj_w"].astype(np.float32, copy=False)),
        "pb_row": np.ascontiguousarray(a["L0_attn_proj_b"].reshape(1, D).astype(np.float32, copy=False)),
        "wg_t": wg_t, "wu_t": wu_t, "wd_t": wd_t,
        "gs86": np.ascontiguousarray(a["L0_ffn_gate_s"].reshape(NF, P).astype(np.float32, copy=False)),
        "us86": np.ascontiguousarray(a["L0_ffn_up_s"].reshape(NF, P).astype(np.float32, copy=False)),
        "ds_row": np.ascontiguousarray(a["L0_ffn_down_s"].reshape(1, D).astype(np.float32, copy=False)),
    }
    return shared, a


def _shard_x(x, c):
    n, h = c // 2, c % 2
    xe = np.zeros((THL, D), np.float32)
    if h == 0:
        xe[1:THL] = x[n, 0:THL - 1]
    else:
        xe[0:THL - 1] = x[n, T - 1:2 * T]
    return xe


def kernel(**inputs):
    shared, a = _prep_shared(inputs)
    x = a["L0_attn_inputs"].astype(np.float32, copy=False)
    nc = build_nc()
    in_maps = []
    for c in range(8):
        m = dict(shared)
        m["x_ext"] = _shard_x(x, c)
        in_maps.append(m)
    res = run_bass_kernel_spmd(nc, in_maps, core_ids=list(range(8)))
    out = np.empty((4, 2048, D), np.float32)
    for c, r in enumerate(res.results):
        n, h = c // 2, c % 2
        out[n, h * T:(h + 1) * T] = r["out"]
    return out


# version 05
# speedup vs baseline: 1.0984x; 1.0984x over previous
import numpy as np
from contextlib import ExitStack

import concourse.bass as bass
import concourse.mybir as mybir
from concourse.bass_utils import run_bass_kernel_spmd

F32 = mybir.dt.float32
BF16 = mybir.dt.bfloat16
I32 = mybir.dt.int32
U8 = mybir.dt.uint8
AF = mybir.ActivationFunctionType
ALU = mybir.AluOpType
AX = mybir.AxisListType

P = 128
T = 1024
THL = 1026
D = 4096
ND = 32
W3 = 3072
NW3 = 24
NWG = 12
WA = 1024
NWA = 8
NH = 16
DH = 64
DFF = 11008
NF = 86
FQ = [(0, 22), (22, 22), (44, 21), (65, 21)]
EPS = 1e-5
ENGINES = ("sync", "gpsimd", "tensor", "vector", "scalar")
ENG2SEM = {"sync": "sy", "gpsimd": "gq", "tensor": "tn",
           "vector": "vc", "scalar": "ac"}


class Builder:
    def __init__(self, nc, ctx):
        self.nc = nc
        self.ctx = ctx
        self.sched = {e: [] for e in ENGINES}
        self.sems = {}
        for name in ("sy", "gq", "tn", "vc", "ac"):
            self.new_sem(name)
        self.hiwater = {}

    def new_sem(self, name):
        h = self.ctx.enter_context(self.nc.semaphore(name))
        self.sems[name] = [h, 0]
        return name

    def emit(self, eng, fn):
        self.sched[eng].append(fn)

    def val(self, s):
        return self.sems[s][1]

    def wait(self, eng, s, v):
        if v is None or v <= 0:
            return
        if v <= self.hiwater.get((eng, s), 0):
            return
        self.hiwater[(eng, s)] = v
        h = self.sems[s][0]
        self.emit(eng, lambda q, h=h, v=v: q.wait_ge(h, v))

    def run(self, eng, fn, inc=None, n=1, fence=False):
        if fence and inc is None:
            inc = ENG2SEM[eng]
        if inc is None:
            self.emit(eng, fn)
            return None
        rec = self.sems[inc]
        rec[1] += n
        h = rec[0]
        self.emit(eng, lambda q, fn=fn, h=h, n=n: fn(q).then_inc(h, n))
        if fence:
            self.wait(eng, inc, rec[1])
        return rec[1]

    def qwait(self, eng, s):
        self.wait(eng, s, self.val(s))

    def dma(self, eng, out_ap, in_ap, sem=None, **kw):
        s = sem if sem else ("sy" if eng == "sync" else "gq")
        rec = self.sems[s]
        rec[1] += 16
        h = rec[0]
        self.emit(eng, lambda q, o=out_ap, i=in_ap, h=h, kw=kw:
                  q.dma_start(out=o, in_=i, **kw).then_inc(h, 16))
        return rec[1]

    def barrier(self):
        for eng, s in ENG2SEM.items():
            self.run(eng, lambda q: q.drain(), inc=s)
        snap = {s: self.val(s) for s in self.sems}
        for eng in ENGINES:
            for s, v in snap.items():
                if s != ENG2SEM[eng]:
                    self.wait(eng, s, v)

    def finalize(self):
        with self.nc.Block() as block:
            for eng in ENGINES:
                def body(q, eng=eng):
                    for f in self.sched[eng]:
                        f(q)
                getattr(block, eng)(body)


def build_nc(debug=False):
    nc = bass.Bass("TRN2", target_bir_lowering=False)

    x_ext = nc.dram_tensor("x_ext", [THL, D], F32, kind="ExternalInput").ap()
    conv_t = nc.dram_tensor("conv_t", [NWG, 3, ND, P, 256], F32, kind="ExternalInput").ap()
    conv_b24 = nc.dram_tensor("conv_b24", [NW3, P], F32, kind="ExternalInput").ap()
    wq2 = nc.dram_tensor("wq2", [WA, WA], F32, kind="ExternalInput").ap()
    wk2 = nc.dram_tensor("wk2", [WA, WA], F32, kind="ExternalInput").ap()
    wv2 = nc.dram_tensor("wv2", [WA, WA], F32, kind="ExternalInput").ap()
    wo2 = nc.dram_tensor("wo2", [WA, WA], F32, kind="ExternalInput").ap()
    bq8 = nc.dram_tensor("bq8", [8, P], F32, kind="ExternalInput").ap()
    bk8 = nc.dram_tensor("bk8", [8, P], F32, kind="ExternalInput").ap()
    bv_row = nc.dram_tensor("bv_row", [1, WA], F32, kind="ExternalInput").ap()
    bo8 = nc.dram_tensor("bo8", [8, P], F32, kind="ExternalInput").ap()
    projw = nc.dram_tensor("projw", [WA, D], F32, kind="ExternalInput").ap()
    pb_row = nc.dram_tensor("pb_row", [1, D], F32, kind="ExternalInput").ap()
    wg_t = nc.dram_tensor("wg_t", [NF, 8, P, 512], I32, kind="ExternalInput").ap()
    wu_t = nc.dram_tensor("wu_t", [NF, 8, P, 512], I32, kind="ExternalInput").ap()
    wd_t = nc.dram_tensor("wd_t", [NF, 8, P, 512], I32, kind="ExternalInput").ap()
    gs86 = nc.dram_tensor("gs86", [NF, P], F32, kind="ExternalInput").ap()
    us86 = nc.dram_tensor("us86", [NF, P], F32, kind="ExternalInput").ap()
    ds_row_d = nc.dram_tensor("ds_row", [1, D], F32, kind="ExternalInput").ap()
    out_d = nc.dram_tensor("out", [T, D], F32, kind="ExternalOutput").ap()
    spill_kind = "ExternalOutput" if debug else "Internal"
    spill = nc.dram_tensor("qkv_spill", [NW3, P, T], BF16, kind=spill_kind).ap()
    if debug:
        dbg_xnt = nc.dram_tensor("dbg_xnt", [ND, P, THL], BF16, kind="ExternalOutput").ap()
        dbg_h0 = nc.dram_tensor("dbg_h0", [22, P, T], BF16, kind="ExternalOutput").ap()
        dbg_heads = nc.dram_tensor("dbg_heads", [32, P, T], BF16, kind="ExternalOutput").ap()
        dbg_ow = nc.dram_tensor("dbg_ow", [8, P, T], BF16, kind="ExternalOutput").ap()
        dbg_att = nc.dram_tensor("dbg_att", [8, P, P], F32, kind="ExternalOutput").ap()
        dbg_q = nc.dram_tensor("dbg_q", [4, T, D], F32, kind="ExternalOutput").ap()

    with ExitStack() as ctx:
        b = Builder(nc, ctx)

        PS = ctx.enter_context(nc.psum_tensor("ps", [P, 4096], F32))
        PSB = PS.bitcast(BF16)

        def bankf(i, n=1):
            return PS[:, i * 512:(i + n) * 512]

        def bankb(i, n=1):
            return PSB[:, i * 1024:(i + n) * 1024]

        AR_BYTES = 212000
        arena = ctx.enter_context(nc.sbuf_tensor("arena", [P, AR_BYTES], U8))
        A32h = arena.bitcast(F32)
        ABFh = arena.bitcast(BF16)
        A32 = A32h.ap()
        ABF = ABFh.ap()

        class Alloc:
            cur = 0

            def take(self, nbytes, align=64):
                self.cur = (self.cur + align - 1) // align * align
                off = self.cur
                self.cur += nbytes
                assert self.cur <= AR_BYTES, f"SBUF arena overflow: {self.cur}"
                return off
        al = Alloc()

        def f32v(off, n, p0=0, pn=P):
            return A32[p0:p0 + pn, off // 4: off // 4 + n]

        def bf16v(off, n, p0=0, pn=P):
            return ABF[p0:p0 + pn, off // 2: off // 2 + n]

        o_identb = al.take(256)
        o_identf = al.take(512)
        o_mask = al.take(512)
        o_maskb = al.take(256)
        o_ones = al.take(512)
        o_rows = al.take(512)
        o_cb = al.take(128)
        o_sbq = al.take(64)
        o_bk = al.take(64)
        o_bo = al.take(64)
        o_gs = al.take(384)
        o_us = al.take(384)
        o_dsb = al.take(16384)
        o_pbb = al.take(16384)
        o_bvb = al.take(4096)
        o_eps = al.take(64)
        o_bn = al.take(256)
        o_agg = al.take(64)
        o_lnt = al.take(64)
        o_rstd = al.take(64)
        o_negmb = al.take(64)
        REG_A = al.take(65664)
        REG_B = al.take(49152)
        REG_C = al.take(49152)
        assert al.cur <= AR_BYTES

        identb = bf16v(o_identb, 128)
        maskb = bf16v(o_maskb, 128)
        identf = f32v(o_identf, 128)
        mask = f32v(o_mask, 128)
        ones1 = f32v(o_ones, 128, 0, 1)

        def xnt(d_):
            return bf16v(REG_A + d_ * 2052, THL)

        def xnt3d(d0, nd_, c0, ncols):
            base = REG_A // 2
            return ABF[:, base:base + 32 * THL] \
                .rearrange("p (c n) -> p c n", n=THL)[:, d0:d0 + nd_, c0:c0 + ncols]

        def qkvT(w):
            return bf16v(REG_B + w * 2048, T)

        def hT(i):
            return bf16v(REG_B + i * 2048, T)

        def col(off, j):
            return A32[0:P, off // 4 + j: off // 4 + j + 1]

        def g_(fn, inc=None, n=1, fence=False):
            return b.run("gpsimd", fn, inc=inc, n=n, fence=fence)

        g_(lambda q: q.memset(identb, 0.0), inc="gq", fence=True)
        g_(lambda q: q.affine_select(out=identb, in_=identb, compare_op=ALU.not_equal,
                                     fill=1.0, base=0, pattern=[[-1, P]], channel_multiplier=1))
        g_(lambda q: q.memset(identf, 0.0), inc="gq", fence=True)
        g_(lambda q: q.affine_select(out=identf, in_=identf, compare_op=ALU.not_equal,
                                     fill=1.0, base=0, pattern=[[-1, P]], channel_multiplier=1))
        g_(lambda q: q.memset(mask, 0.0), inc="gq", fence=True)
        g_(lambda q: q.affine_select(out=mask, in_=mask, compare_op=ALU.is_ge,
                                     fill=-1e9, base=0, pattern=[[-1, P]], channel_multiplier=1))
        g_(lambda q: q.memset(maskb, 0.0), inc="gq", fence=True)
        g_(lambda q: q.affine_select(out=maskb, in_=maskb, compare_op=ALU.is_ge,
                                     fill=-1e9, base=0, pattern=[[-1, P]], channel_multiplier=1))
        g_(lambda q: q.memset(ones1, 1.0))
        g_(lambda q: q.memset(f32v(o_eps, 1), EPS))
        setup_ms = g_(lambda q: q.drain(), inc="gq", n=1)

        rowbuf = lambda k: f32v(o_rows, 128, 0, k)

        def vec_to_cols(dram_ap, k, dest_off):
            t_ld = b.dma("sync", rowbuf(k), dram_ap)
            b.wait("tensor", "sy", t_ld)
            b.wait("tensor", "gq", setup_ms)
            ms = b.run("tensor", lambda q, k=k: q.transpose(
                PS[0:P, 0:k], rowbuf(k), identf[0:k, 0:k]), inc="tn")
            b.wait("vector", "tn", ms)
            b.run("vector", lambda q, k=k, d=dest_off: q.tensor_copy(
                A32[0:P, d // 4:d // 4 + k], PS[0:P, 0:k]), inc="vc")
            ev = b.run("vector", lambda q: q.drain(), inc="vc")
            b.wait("tensor", "vc", ev)
            b.wait("sync", "tn", ms)
            return ev

        vec_to_cols(conv_b24[:, :], NW3, o_cb)
        vec_to_cols(bq8[:, :], 8, o_sbq)
        vec_to_cols(bk8[:, :], 8, o_bk)
        vec_to_cols(bo8[:, :], 8, o_bo)
        vec_to_cols(gs86[:, :], NF, o_gs)
        ev_us = vec_to_cols(us86[:, :], NF, o_us)
        b.wait("scalar", "vc", ev_us)
        sbq_ms = b.run("scalar", lambda q: q.mul(
            A32[0:P, o_sbq // 4:o_sbq // 4 + 8],
            A32[0:P, o_sbq // 4:o_sbq // 4 + 8], 0.125), inc="ac")

        o_dsrow = REG_C
        o_pbrow = REG_C + 16384
        o_bvrow = REG_C + 32768
        b.new_sem("rw")
        t1 = b.dma("sync", f32v(o_dsrow, D, 0, 1), ds_row_d, sem="rw")
        t2 = b.dma("sync", f32v(o_pbrow, D, 0, 1), pb_row, sem="rw")
        t3 = b.dma("sync", f32v(o_bvrow, WA, 0, 1), bv_row, sem="rw")
        b.wait("tensor", "rw", t3)
        for src_off, dst_off, n in ((o_dsrow, o_dsb, D), (o_pbrow, o_pbb, D),
                                    (o_bvrow, o_bvb, WA)):
            for j in range(n // 512):
                ms = b.run("tensor", lambda q, s=src_off, j=j: q.matmul(
                    bankf(1), ones1, A32[0:1, s // 4 + j * 512: s // 4 + (j + 1) * 512],
                    start=True, stop=True), inc="tn")
                b.wait("vector", "tn", ms)
                b.run("vector", lambda q, d=dst_off, j=j: q.tensor_copy(
                    A32[0:P, d // 4 + j * 512: d // 4 + (j + 1) * 512], bankf(1)), inc="vc")
                ev = b.run("vector", lambda q: q.drain(), inc="vc")
                b.wait("tensor", "vc", ev)

        b.barrier()

        o_xring = REG_B
        o_xnring = REG_B + 32768
        bn = f32v(o_bn, 48)
        agg = f32v(o_agg, 2)
        b.new_sem("xr0"); b.new_sem("xr1")
        xn_ms = {}
        trp_ms = {}
        ln_ld = {}
        ev_ms = {}
        for t_ in range(9):
            rt = 128 if t_ < 8 else 2
            xv = f32v(o_xring + (t_ % 2) * 16384, D, 0, rt)
            if t_ >= 2:
                b.wait("sync", "vc", xn_ms[t_ - 2])
            ln_ld[t_] = b.dma("sync", xv, x_ext[t_ * P: t_ * P + rt, :],
                              sem=f"xr{t_ % 2}")
            b.wait("vector", f"xr{t_ % 2}", ln_ld[t_])
            for i in range(8):
                b.run("vector", lambda q, xv=xv, i=i, rt=rt: q.bn_stats(
                    bn[0:rt, 6 * i:6 * i + 6], xv[0:rt, 512 * i:512 * (i + 1)]),
                    inc="vc" if i == 7 else None, fence=(i == 7))
            st_ms = b.run("vector", lambda q, rt=rt: q.bn_aggr(
                agg[0:rt, 0:2], bn[0:rt, 0:48]), inc="vc")
            b.wait("scalar", "vc", st_ms)
            b.run("scalar", lambda q, rt=rt: q.activation(
                f32v(o_lnt, 1, 0, rt), agg[0:rt, 1:2], AF.Ln,
                bias=f32v(o_eps, 1, 0, rt), scale=1.0), fence=True)
            rs_ms = b.run("scalar", lambda q, rt=rt: q.activation(
                f32v(o_rstd, 1, 0, rt), f32v(o_lnt, 1, 0, rt), AF.Exp,
                bias=0.0, scale=-0.5), inc="ac")
            b.wait("vector", "ac", rs_ms)
            nm_ms = b.run("vector", lambda q, rt=rt: q.scalar_tensor_tensor(
                f32v(o_negmb, 1, 0, rt), agg[0:rt, 0:1], -1.0,
                f32v(o_rstd, 1, 0, rt), op0=ALU.mult, op1=ALU.mult),
                inc="vc", fence=True)
            if t_ >= 2:
                b.wait("vector", "tn", trp_ms[t_ - 2])
            xnv = bf16v(o_xnring + (t_ % 2) * 8192, D, 0, rt)
            b.run("vector", lambda q, xv=xv, xnv=xnv, rt=rt: q.tensor_scalar(
                xnv, xv, f32v(o_rstd, 1, 0, rt), f32v(o_negmb, 1, 0, rt),
                op0=ALU.mult, op1=ALU.add), inc="vc", fence=True)
            b.run("vector", lambda q, xv=xv, rt=rt: q.tensor_tensor(
                xv, xv, f32v(o_pbb, D, 0, rt), op=ALU.add), inc="vc", fence=True)
            xn_ms[t_] = b.run("vector", lambda q: q.drain(), inc="vc")
            b.wait("sync", "vc", xn_ms[t_])
            if t_ == 0:
                b.dma("sync", out_d[0:127, :], f32v(o_xring + 0, D, 1, 127))
            elif t_ < 8:
                b.dma("sync", out_d[t_ * P - 1: (t_ + 1) * P - 1, :], xv)
            else:
                b.dma("sync", out_d[T - 1: T, :], f32v(o_xring + (t_ % 2) * 16384, D, 0, 1))
            b.wait("tensor", "vc", xn_ms[t_])
            for g in range(8):
                gg = t_ * 8 + g
                bk_ = 2 + gg % 2
                if gg >= 2:
                    b.wait("tensor", "vc", ev_ms[gg - 2])
                for j in range(4):
                    d_ = g * 4 + j
                    last = (j == 3)
                    m = b.run("tensor", lambda q, bk_=bk_, j=j, d_=d_, xnv=xnv, rt=rt: q.transpose(
                        bankb(bk_)[:, j * 128:j * 128 + rt],
                        xnv[0:rt, d_ * 128:(d_ + 1) * 128], identb[0:rt, 0:rt]),
                        inc="tn" if last else None)
                    if last:
                        grp_ms = m
                if g == 7:
                    trp_ms[t_] = grp_ms
                b.wait("vector", "tn", grp_ms)
                src3 = bankb(bk_).rearrange("p (c n) -> p c n", n=128)[:, 0:4, 0:rt]
                ev_ms[gg] = b.run("vector", lambda q, g=g, t_=t_, rt=rt, src3=src3:
                                  q.tensor_copy(xnt3d(g * 4, 4, t_ * 128, rt), src3),
                                  inc="vc")
        b.barrier()
        if debug:
            for d_ in range(ND):
                b.dma("sync", dbg_xnt[d_, :, :], xnt(d_))
            b.barrier()

        o_cring = REG_C
        CR = 12
        for s_ in range(CR):
            b.new_sem(f"cr{s_}")
        cons = []
        evac_ms = {}
        for wg in range(NWG):
            base = 0 if wg % 2 == 0 else 4
            if wg >= 2:
                b.wait("tensor", "ac", evac_ms[wg - 2])
            for tap in range(3):
                for d_ in range(ND):
                    gi = wg * 96 + tap * 32 + d_
                    slot = gi % CR
                    if gi >= CR:
                        b.wait("gpsimd", "tn", cons[gi - CR])
                    ld = b.dma("gpsimd", bf16v(o_cring + slot * 512, 256),
                               conv_t[wg, tap, d_, :, :], sem=f"cr{slot}")
                    b.wait("tensor", f"cr{slot}", ld)
                    first = (tap == 0 and d_ == 0)
                    lastk = (tap == 2 and d_ == ND - 1)
                    for wt in range(2):
                        for c in range(2):
                            bk_ = base + wt * 2 + c
                            is_last_of_tile = (wt == 1 and c == 1)
                            b.run("tensor", lambda q, bk_=bk_, slot=slot, wt=wt, c=c,
                                  d_=d_, tap=tap, first=first, lastk=lastk: q.matmul(
                                bankf(bk_),
                                bf16v(o_cring + slot * 512, 256)[:, wt * 128:(wt + 1) * 128],
                                xnt(d_)[:, c * 512 + tap: c * 512 + tap + 512],
                                start=first, stop=lastk),
                                inc="tn" if is_last_of_tile else None)
                    cons.append(b.val("tn"))
            stop_all = cons[-1]
            b.wait("scalar", "tn", stop_all)
            for wt in range(2):
                for c in range(2):
                    w_ = wg * 2 + wt
                    e = b.run("scalar", lambda q, base=base, wt=wt, c=c, w_=w_: q.activation(
                        qkvT(w_)[:, c * 512:(c + 1) * 512], bankf(base + wt * 2 + c),
                        AF.Identity, bias=col(o_cb, w_), scale=1.0), inc="ac")
            evac_ms[wg] = e
            e = b.run("scalar", lambda q: q.drain(), inc="ac")
            b.wait("sync", "ac", e)
            for wt in range(2):
                w_ = wg * 2 + wt
                b.dma("sync", spill[w_, :, :], qkvT(w_))
        b.barrier()

        o_sil = REG_C
        o_gring = REG_C + 8192
        o_uring = REG_C + 19456
        o_wdring = REG_C + 30720
        o_ftmp = REG_C + 36864
        o_pin = REG_C + 40960
        GR = 11
        WR = 6
        for s_ in range(GR):
            b.new_sem(f"gr{s_}")
            b.new_sem(f"ur{s_}")
        for s_ in range(WR):
            b.new_sem(f"wr{s_}")
        gslotv = lambda s: bf16v(o_gring + s * 1024, 512)
        uslotv = lambda s: bf16v(o_uring + s * 1024, 512)
        wdslotv = lambda s: bf16v(o_wdring + s * 1024, 512)
        ftmpv = lambda s: f32v(o_ftmp + s * 2048, 512)
        pinv = lambda s: f32v(o_pin + s * 2048, 512)
        eslotv = lambda s: f32v(o_gring + s * 2048, 512)
        eslot_rel = {}

        for s_ in range(4):
            b.new_sem(f"pn{s_}")
        gu_cons = []
        gu_loads = []
        flush_ms = {}
        for qf, (f0, nf_) in enumerate(FQ):
            sil_ms = {}
            h_ms = {}
            for fi in range(nf_):
                ft = f0 + fi
                par = fi % 2
                pgate = PS[:, (par * 2048):(par * 2048 + 1024)]
                pup = PS[:, (par * 2048 + 1024):(par * 2048 + 2048)]
                if fi >= 2:
                    b.wait("tensor", "ac", sil_ms[fi - 2])
                    b.wait("tensor", "vc", h_ms[fi - 2])
                ldg = [None] * 8
                ldu = [None] * 8
                for dg in range(8):
                    gi = len(gu_loads)
                    if gi >= 2 * GR:
                        b.wait("gpsimd", "tn", gu_cons[gi - 2 * GR])
                    sl = (fi * 8 + dg) % GR
                    ldg[dg] = (sl, b.dma("gpsimd", gslotv(sl), wg_t[ft, dg, :, :],
                                          sem=f"gr{sl}"))
                    gu_loads.append(ldg[dg]); gu_cons.append(None)
                    gi = len(gu_loads)
                    if gi >= 2 * GR:
                        b.wait("gpsimd", "tn", gu_cons[gi - 2 * GR])
                    ldu[dg] = (sl, b.dma("gpsimd", uslotv(sl), wu_t[ft, dg, :, :],
                                          sem=f"ur{sl}"))
                    gu_loads.append(ldu[dg]); gu_cons.append(None)
                for d_ in range(ND):
                    dg, j = d_ // 4, d_ % 4
                    sl = (fi * 8 + dg) % GR
                    b.wait("tensor", f"gr{sl}", ldg[dg][1])
                    b.wait("tensor", f"ur{sl}", ldu[dg][1])
                    first = (d_ == 0)
                    lastk = (d_ == ND - 1)
                    for mat, psum, ring in ((0, pgate, gslotv(sl)), (1, pup, uslotv(sl))):
                        for c in range(2):
                            do_inc = (mat == 1 and c == 1 and j == 3)
                            b.run("tensor", lambda q, psum=psum, ring=ring, j=j, c=c,
                                  d_=d_, first=first, lastk=lastk: q.matmul(
                                psum[:, c * 512:(c + 1) * 512],
                                ring[:, j * 128:(j + 1) * 128],
                                xnt(d_)[:, 1 + c * 512: 1 + c * 512 + 512],
                                start=first, stop=lastk),
                                inc="tn" if do_inc else None)
                    if j == 3:
                        v = b.val("tn")
                        gu_cons[len(gu_loads) - 16 + dg * 2] = v
                        gu_cons[len(gu_loads) - 16 + dg * 2 + 1] = v
                stop_ms = b.val("tn")
                b.wait("scalar", "tn", stop_ms)
                if fi >= 2:
                    b.wait("scalar", "vc", h_ms[fi - 2])
                sil_ms[fi] = b.run("scalar", lambda q, par=par, pgate=pgate, ft=ft: q.activation(
                    f32v(o_sil + par * 4096, 1024), pgate, AF.Silu,
                    bias=0.0, scale=col(o_gs, ft)), inc="ac")
                b.wait("vector", "ac", sil_ms[fi])
                b.wait("vector", "tn", stop_ms)
                h_ms[fi] = b.run("vector", lambda q, fi=fi, par=par, pup=pup, ft=ft: q.scalar_tensor_tensor(
                    hT(fi), pup, col(o_us, ft), f32v(o_sil + par * 4096, 1024),
                    op0=ALU.mult, op1=ALU.mult), inc="vc")
            last_sil = sil_ms[nf_ - 1]
            last_h = h_ms[nf_ - 1]
            wd_cons = []
            wd_loads = []
            b.wait("tensor", "ac", last_sil)
            b.wait("tensor", "vc", last_h)
            for dt in range(8):
                bank_stop = [None] * 8
                pin_ld = {ts: b.dma("sync", pinv(ts % 4), out_d[
                    ts * P:(ts + 1) * P, dt * 512:(dt + 1) * 512],
                    sem=f"pn{ts % 4}")
                    for ts in range(4)}
                for fi in range(nf_):
                    ft = f0 + fi
                    gi = len(wd_loads)
                    if gi >= WR:
                        b.wait("gpsimd", "tn", wd_cons[gi - WR])
                    sl = gi % WR
                    ld = b.dma("gpsimd", wdslotv(sl), wd_t[ft, dt, :, :],
                               sem=f"wr{sl}")
                    wd_loads.append(ld)
                    b.wait("tensor", f"wr{sl}", ld)
                    for ts in range(8):
                        if fi == 0 and dt > 0:
                            b.wait("tensor", *flush_ms[(dt - 1, ts)])
                        do_inc = (ts == 7) or (fi == nf_ - 1)
                        m = b.run("tensor", lambda q, ts=ts, fi=fi, sl=sl, nf_=nf_: q.matmul(
                            bankf(ts), hT(fi)[:, ts * 128:(ts + 1) * 128], wdslotv(sl),
                            start=(fi == 0), stop=(fi == nf_ - 1)),
                            inc="tn" if do_inc else None)
                        if fi == nf_ - 1:
                            bank_stop[ts] = b.val("tn")
                    wd_cons.append(b.val("tn"))
                for ts in range(8):
                    eng = "scalar" if ts % 2 else "vector"
                    sem = "ac" if ts % 2 else "vc"
                    b.wait(eng, "tn", bank_stop[ts])
                    b.wait(eng, "vc", eslot_rel.get(ts, 0))
                    m = b.run(eng, lambda q, ts=ts, eng=eng:
                              (q.copy if eng == "scalar" else q.tensor_copy)(
                                  eslotv(ts), bankf(ts)), inc=sem, fence=True)
                    flush_ms[(dt, ts)] = (sem, m)
                for ts in range(8):
                    sl = ts % 4
                    if ts >= 4:
                        pin_ld[ts] = b.dma("sync", pinv(sl), out_d[
                            ts * P:(ts + 1) * P, dt * 512:(dt + 1) * 512],
                            sem=f"pn{sl}")
                    sem, m = flush_ms[(dt, ts)]
                    b.wait("vector", sem, m)
                    b.wait("vector", f"pn{sl}", pin_ld[ts])
                    eslot_rel[ts] = b.run("vector", lambda q, ts=ts, dt=dt: q.tensor_tensor(
                        ftmpv(ts % 2), eslotv(ts),
                        A32[0:P, o_dsb // 4 + dt * 512: o_dsb // 4 + (dt + 1) * 512],
                        op=ALU.mult), inc="vc", fence=True)
                    fm = b.run("vector", lambda q, ts=ts, sl=sl: q.tensor_tensor(
                        pinv(sl), ftmpv(ts % 2), pinv(sl), op=ALU.add), inc="vc")
                    b.wait("sync", "vc", fm)
                    b.dma("sync", out_d[ts * P:(ts + 1) * P, dt * 512:(dt + 1) * 512],
                          pinv(sl))
                    if debug:
                        b.dma("sync", dbg_q[qf, ts * P:(ts + 1) * P,
                                            dt * 512:(dt + 1) * 512], pinv(sl))
            if debug and qf == 0:
                b.wait("sync", "vc", last_h)
                for fi in range(nf_):
                    b.dma("sync", dbg_h0[fi, :, :], hT(fi))
            b.barrier()
            flush_ms = {}

        qhT = lambda ht: bf16v(REG_A + ht * 2048, T)
        khT = lambda ht: bf16v(REG_A + 16384 + ht * 2048, T)
        vh = lambda ts: bf16v(REG_A + 32768 + ts * 2048, T)
        oT = lambda ht: bf16v(REG_A + 49152 + ht * 2048, T)
        owT = lambda wt: bf16v(REG_C + wt * 2048, T)
        hw = lambda wc: bf16v(REG_C + 16384 + wc * 2048, T)
        o_pring = REG_C + 32768
        o_pin3 = REG_C + 35840
        o_sm = REG_C + 41984
        o_pf = o_sm + 2048
        o_pbf = o_pf + 2048
        o_ptb = o_pbf + 1024
        o_nm = o_ptb + 512
        o_ssum = o_nm + 128
        o_rcp = o_ssum + 128
        pring = lambda s: bf16v(o_pring + s * 1024, 512)
        pin3 = lambda s: f32v(o_pin3 + s * 2048, 512)
        for s_ in range(3):
            b.new_sem(f"pr{s_}")
            b.new_sem(f"pq{s_}")
        for s_ in range(NWA):
            b.new_sem(f"hw{s_}")
        b.new_sem("qk")

        for w in range(NW3):
            b.dma("sync", qkvT(w), spill[w, :, :], sem="qk")
        qk_all = b.val("qk")

        def head_proj(wsrc, kind):
            hld = [b.dma("gpsimd", hw(wc), wsrc[wc * P:(wc + 1) * P, :],
                         sem=f"hw{wc}") for wc in range(NWA)]
            ev = {}
            if kind == "v":
                for ts in range(8):
                    par = ts % 2
                    ps = PS[:, par * 2048: par * 2048 + 1024]
                    if ts >= 2:
                        b.wait("tensor", "vc", ev[ts - 2])
                    for wc in range(NWA):
                        b.wait("tensor", f"hw{wc}", hld[wc])
                        b.wait("tensor", "qk", qk_all)
                        for c in range(2):
                            do_inc = (wc == NWA - 1 and c == 1)
                            b.run("tensor", lambda q, ps=ps, wc=wc, c=c, ts=ts: q.matmul(
                                ps[:, c * 512:(c + 1) * 512],
                                qkvT(16 + wc)[:, ts * 128:(ts + 1) * 128],
                                hw(wc)[:, c * 512:(c + 1) * 512],
                                start=(wc == 0), stop=(wc == NWA - 1)),
                                inc="tn" if do_inc else None)
                    sm_ = b.val("tn")
                    b.wait("vector", "tn", sm_)
                    ev[ts] = b.run("vector", lambda q, ts=ts, ps=ps: q.tensor_tensor(
                        vh(ts), ps, A32[0:P, o_bvb // 4: o_bvb // 4 + 1024],
                        op=ALU.add), inc="vc")
                return ev[7]
            else:
                src0 = 0 if kind == "q" else 8
                dest = qhT if kind == "q" else khT
                bias_off = o_sbq if kind == "q" else o_bk
                scale = 0.125 if kind == "q" else 1.0
                for ht in range(NWA):
                    par = ht % 2
                    ps = PS[:, par * 2048: par * 2048 + 1024]
                    if ht >= 2:
                        b.wait("tensor", "ac", ev[ht - 2])
                    for wc in range(NWA):
                        b.wait("tensor", f"hw{wc}", hld[wc])
                        b.wait("tensor", "qk", qk_all)
                        for c in range(2):
                            do_inc = (wc == NWA - 1 and c == 1)
                            b.run("tensor", lambda q, ps=ps, wc=wc, c=c, ht=ht, src0=src0: q.matmul(
                                ps[:, c * 512:(c + 1) * 512],
                                hw(wc)[:, ht * 128:(ht + 1) * 128],
                                qkvT(src0 + wc)[:, c * 512:(c + 1) * 512],
                                start=(wc == 0), stop=(wc == NWA - 1)),
                                inc="tn" if do_inc else None)
                    sm_ = b.val("tn")
                    b.wait("scalar", "tn", sm_)
                    ev[ht] = b.run("scalar", lambda q, ht=ht, ps=ps, dest=dest,
                                   bias_off=bias_off, scale=scale: q.activation(
                        dest(ht), ps, AF.Identity, bias=col(bias_off, ht), scale=scale),
                        inc="ac")
                return ev[7]

        head_proj(wq2, "q")
        b.barrier()
        head_proj(wk2, "k")
        b.barrier()
        head_proj(wv2, "v")
        b.barrier()
        if debug:
            for j in range(8):
                b.dma("sync", dbg_heads[j, :, :], qhT(j))
                b.dma("sync", dbg_heads[8 + j, :, :], khT(j))
                b.dma("sync", dbg_heads[16 + j, :, :], vh(j))
            b.barrier()

        sm_ms = {}
        exp_ms = {}
        rcp_ms = {}
        pbf_ms = {}
        pt_ms = {}
        ptev_ms = {}
        pv_ms = {}
        oev_ms = {}
        sum_ms4 = {}
        bankrel = {}
        for blk in range(8):
            for h in range(NH):
                i = blk * NH + h
                par = i % 4
                ip = i // 2
                par2 = ip % 2
                ht = h // 2
                hrow = (h % 2) * 64
                ps_s = PS[:, par * 512: par * 512 + 128]
                if i >= 4:
                    b.wait("tensor", bankrel[i - 4][0], bankrel[i - 4][1])
                b.run("tensor", lambda q, ps_s=ps_s: q.matmul(
                    ps_s, identb, maskb, start=True, stop=False))
                b.run("tensor", lambda q, ps_s=ps_s, ht=ht, hrow=hrow, blk=blk: q.matmul(
                    ps_s,
                    qhT(ht)[hrow:hrow + 64, blk * 128:(blk + 1) * 128],
                    khT(ht)[hrow:hrow + 64, blk * 128:(blk + 1) * 128],
                    start=False, stop=True), inc="tn")
                s_ms = b.val("tn")
                b.wait("vector", "tn", s_ms)
                if i >= 4:
                    b.wait("vector", "ac", exp_ms[i - 4])
                sm_ms[i] = b.run("vector", lambda q, ps_s=ps_s, par=par: q.reduce_max(
                    f32v(o_nm + par * 32, 1), ps_s, axis=AX.X, negate=True), inc="vc")
                b.wait("scalar", "vc", sm_ms[i])
                if i >= 4:
                    b.wait("scalar", "vc", sum_ms4[i - 4])
                pfv = f32v(o_pf + par * 512, 128)
                exp_ms[i] = b.run("scalar", lambda q, pfv=pfv, ps_s=ps_s, par=par: q.activation(
                    pfv, ps_s, AF.Exp, bias=f32v(o_nm + par * 32, 1), scale=1.0), inc="ac")
                bankrel[i] = ("ac", exp_ms[i])
                b.wait("vector", "ac", exp_ms[i])
                if i >= 4:
                    b.wait("vector", "ac", rcp_ms[i - 4])
                sum_ms4[i] = b.run("vector", lambda q, pfv=pfv, par=par: q.reduce_sum(
                    f32v(o_ssum + par * 32, 1), pfv, axis=AX.X), inc="vc")
                b.wait("scalar", "vc", sum_ms4[i])
                b.run("scalar", lambda q, par=par: q.activation(
                    f32v(o_rcp + par * 32, 1), f32v(o_ssum + par * 32, 1),
                    AF.Ln, bias=0.0, scale=1.0), fence=True)
                rcp_ms[i] = b.run("scalar", lambda q, par=par: q.activation(
                    f32v(o_rcp + par * 32, 1), f32v(o_rcp + par * 32, 1),
                    AF.Exp, bias=0.0, scale=-1.0), inc="ac", fence=True)
                pbfv = bf16v(o_pbf + par * 256, 128)
                if i >= 4:
                    b.wait("scalar", "tn", pt_ms[i - 4])
                pbf_ms[i] = b.run("scalar", lambda q, pbfv=pbfv, pfv=pfv, par=par: q.activation(
                    pbfv, pfv, AF.Identity, bias=0.0,
                    scale=f32v(o_rcp + par * 32, 1)), inc="ac")
                b.wait("tensor", "ac", pbf_ms[i])
                if i >= 2:
                    b.wait("tensor", "vc", ptev_ms[i - 2])
                b.run("tensor", lambda q, i=i, pbfv=pbfv: q.transpose(
                    bankb(4 + i % 2)[:, 0:128], pbfv, identb), inc="tn")
                pt_ms[i] = b.val("tn")
                b.wait("vector", "tn", pt_ms[i])
                ptbv = bf16v(o_ptb + (i % 2) * 256, 128)
                if i >= 2:
                    b.wait("vector", "tn", pv_ms[i - 2])
                ptev_ms[i] = b.run("vector", lambda q, ptbv=ptbv, i=i: q.tensor_copy(
                    ptbv, bankb(4 + i % 2)[:, 0:128]), inc="vc")
                if debug and i < 2:
                    dsl = i * 4
                    b.wait("vector", "ac", exp_ms[i])
                    c1 = b.run("vector", lambda q, dsl=dsl, ps_s=ps_s: q.tensor_copy(
                        f32v(o_sil + 2048 * (dsl % 2), 128), ps_s), inc="vc")
                    bankrel[i] = ("vc", c1)
                    b.wait("sync", "vc", c1)
                    b.dma("sync", dbg_att[dsl, :, :], f32v(o_sil + 2048 * (dsl % 2), 128))
                    b.dma("sync", dbg_att[dsl + 1, :, :], f32v(o_sil + 2048 * (dsl % 2), 128))
                    b.dma("sync", dbg_att[dsl + 2, :, :], pfv)
                    b.wait("sync", "tn", pt_ms[i])
                    c2 = b.run("vector", lambda q, dsl=dsl, i=i: q.tensor_copy(
                        f32v(o_sil + 2048 * (dsl % 2), 128), bankb(4 + i % 2)[:, 0:128]), inc="vc")
                    ptev_ms[i] = c2
                    b.wait("sync", "vc", c2)
                    b.dma("sync", dbg_att[dsl + 3, :, :], f32v(o_sil + 2048 * (dsl % 2), 128))
                b.wait("tensor", "vc", ptev_ms[i])
                if h % 2 == 0 and ip >= 2:
                    b.wait("tensor", "vc", oev_ms[ip - 2])
                b.run("tensor", lambda q, hrow=hrow, par2=par2, blk=blk, h=h, ptbv=ptbv: q.matmul(
                    PS[hrow:hrow + 64, (6 + par2) * 512: (6 + par2) * 512 + 128],
                    vh(blk)[:, h * 64:(h + 1) * 64], ptbv,
                    start=True, stop=True), inc="tn")
                pv_ms[i] = b.val("tn")
                if h % 2 == 1:
                    op_ms = pv_ms[i]
                    b.wait("vector", "tn", op_ms)
                    oev_ms[ip] = b.run("vector", lambda q, ht=ht, blk=blk, par2=par2: q.tensor_copy(
                        oT(ht)[:, blk * 128:(blk + 1) * 128],
                        PS[:, (6 + par2) * 512: (6 + par2) * 512 + 128]), inc="vc")
        b.barrier()
        if debug:
            for j in range(8):
                b.dma("sync", dbg_heads[24 + j, :, :], oT(j))
            b.barrier()

        hld = [b.dma("gpsimd", hw(wc), wo2[wc * P:(wc + 1) * P, :],
                     sem=f"hw{wc}") for wc in range(NWA)]
        evo = {}
        for wt in range(NWA):
            par = wt % 2
            ps = PS[:, par * 2048: par * 2048 + 1024]
            if wt >= 2:
                b.wait("tensor", "ac", evo[wt - 2])
            for wc in range(NWA):
                b.wait("tensor", f"hw{wc}", hld[wc])
                for c in range(2):
                    do_inc = (wc == NWA - 1 and c == 1)
                    b.run("tensor", lambda q, ps=ps, wc=wc, c=c, wt=wt: q.matmul(
                        ps[:, c * 512:(c + 1) * 512],
                        hw(wc)[:, wt * 128:(wt + 1) * 128],
                        oT(wc)[:, c * 512:(c + 1) * 512],
                        start=(wc == 0), stop=(wc == NWA - 1)),
                        inc="tn" if do_inc else None)
            sm_ = b.val("tn")
            b.wait("scalar", "tn", sm_)
            evo[wt] = b.run("scalar", lambda q, wt=wt, ps=ps: q.activation(
                owT(wt), ps, AF.Identity, bias=col(o_bo, wt), scale=1.0), inc="ac")
        b.barrier()
        if debug:
            for j in range(8):
                b.dma("sync", dbg_ow[j, :, :], owT(j))
            b.barrier()

        pw_cons = []
        pw_loads = []
        fin_ms = {}
        ptmp_st = {}
        for dt in range(8):
            bank_stop = [None] * 8
            pin_ld = {ts: b.dma("sync", pin3(ts % 3), out_d[
                ts * P:(ts + 1) * P, dt * 512:(dt + 1) * 512],
                sem=f"pq{ts % 3}")
                for ts in range(3)}
            for wc in range(NWA):
                gi = len(pw_loads)
                if gi >= 3:
                    b.wait("gpsimd", "tn", pw_cons[gi - 3])
                sl = gi % 3
                ld = b.dma("gpsimd", pring(sl), projw[wc * P:(wc + 1) * P, dt * 512:(dt + 1) * 512],
                           sem=f"pr{sl}")
                pw_loads.append(ld)
                b.wait("tensor", f"pr{sl}", ld)
                for ts in range(8):
                    if wc == 0 and dt > 0:
                        b.wait("tensor", "vc", fin_ms[(dt - 1, ts)])
                    do_inc = (ts == 7) or (wc == NWA - 1)
                    b.run("tensor", lambda q, ts=ts, wc=wc, sl=sl: q.matmul(
                        bankf(ts), owT(wc)[:, ts * 128:(ts + 1) * 128], pring(sl),
                        start=(wc == 0), stop=(wc == NWA - 1)),
                        inc="tn" if do_inc else None)
                    if wc == NWA - 1:
                        bank_stop[ts] = b.val("tn")
                pw_cons.append(b.val("tn"))
            for ts in range(8):
                sl = ts % 3
                if ts >= 3:
                    pin_ld[ts] = b.dma("sync", pin3(sl), out_d[
                        ts * P:(ts + 1) * P, dt * 512:(dt + 1) * 512],
                        sem=f"pq{sl}")
                b.wait("vector", "tn", bank_stop[ts])
                b.wait("vector", f"pq{sl}", pin_ld[ts])
                fin_ms[(dt, ts)] = b.run("vector", lambda q, ts=ts, sl=sl: q.tensor_tensor(
                    pin3(sl), bankf(ts), pin3(sl), op=ALU.add), inc="vc")
                b.wait("sync", "vc", fin_ms[(dt, ts)])
                b.dma("sync", out_d[ts * P:(ts + 1) * P, dt * 512:(dt + 1) * 512],
                      pin3(sl))
        b.wait("sync", "gq", b.val("gq"))
        b.wait("sync", "sy", b.val("sy"))

        b.finalize()
    return nc


def _prep_shared(inputs):
    a = {k: np.asarray(v) for k, v in inputs.items()}
    conv_w = a["L0_attn_conv_w"].astype(np.float32, copy=False)
    conv_t = np.ascontiguousarray(
        conv_w.reshape(3, ND, P, NWG, 256).transpose(3, 0, 1, 2, 4))
    gw = a["L0_ffn_gate_w"]
    uw = a["L0_ffn_up_w"]
    dw = a["L0_ffn_down_w"]
    wgT = np.ascontiguousarray(gw.T)
    wuT = np.ascontiguousarray(uw.T)
    wdT = np.ascontiguousarray(dw.T)
    wg_t = np.ascontiguousarray(
        wgT.reshape(ND, P, NF, P).transpose(2, 0, 1, 3)
           .reshape(NF, 8, 4, P, P).transpose(0, 1, 3, 2, 4).reshape(NF, 8, P, 512))
    wu_t = np.ascontiguousarray(
        wuT.reshape(ND, P, NF, P).transpose(2, 0, 1, 3)
           .reshape(NF, 8, 4, P, P).transpose(0, 1, 3, 2, 4).reshape(NF, 8, P, 512))
    wd_t = np.ascontiguousarray(
        wdT.reshape(NF, P, 8, 512).transpose(0, 2, 1, 3))
    shared = {
        "conv_t": conv_t,
        "conv_b24": np.ascontiguousarray(a["L0_attn_conv_b"].reshape(NW3, P).astype(np.float32, copy=False)),
        "wq2": np.ascontiguousarray(a["L0_attn_wq"].reshape(WA, WA).astype(np.float32, copy=False)),
        "wk2": np.ascontiguousarray(a["L0_attn_wk"].reshape(WA, WA).astype(np.float32, copy=False)),
        "wv2": np.ascontiguousarray(a["L0_attn_wv"].reshape(WA, WA).astype(np.float32, copy=False)),
        "wo2": np.ascontiguousarray(a["L0_attn_wo"].reshape(WA, WA).astype(np.float32, copy=False)),
        "bq8": np.ascontiguousarray(a["L0_attn_bq"].reshape(8, P).astype(np.float32, copy=False)),
        "bk8": np.ascontiguousarray(a["L0_attn_bk"].reshape(8, P).astype(np.float32, copy=False)),
        "bv_row": np.ascontiguousarray(a["L0_attn_bv"].reshape(1, WA).astype(np.float32, copy=False)),
        "bo8": np.ascontiguousarray(a["L0_attn_bo"].reshape(8, P).astype(np.float32, copy=False)),
        "projw": np.ascontiguousarray(a["L0_attn_proj_w"].astype(np.float32, copy=False)),
        "pb_row": np.ascontiguousarray(a["L0_attn_proj_b"].reshape(1, D).astype(np.float32, copy=False)),
        "wg_t": wg_t, "wu_t": wu_t, "wd_t": wd_t,
        "gs86": np.ascontiguousarray(a["L0_ffn_gate_s"].reshape(NF, P).astype(np.float32, copy=False)),
        "us86": np.ascontiguousarray(a["L0_ffn_up_s"].reshape(NF, P).astype(np.float32, copy=False)),
        "ds_row": np.ascontiguousarray(a["L0_ffn_down_s"].reshape(1, D).astype(np.float32, copy=False)),
    }
    return shared, a


def _shard_x(x, c):
    n, h = c // 2, c % 2
    xe = np.zeros((THL, D), np.float32)
    if h == 0:
        xe[1:THL] = x[n, 0:THL - 1]
    else:
        xe[0:THL - 1] = x[n, T - 1:2 * T]
    return xe


def kernel(**inputs):
    shared, a = _prep_shared(inputs)
    x = a["L0_attn_inputs"].astype(np.float32, copy=False)
    nc = build_nc()
    in_maps = []
    for c in range(8):
        m = dict(shared)
        m["x_ext"] = _shard_x(x, c)
        in_maps.append(m)
    res = run_bass_kernel_spmd(nc, in_maps, core_ids=list(range(8)))
    out = np.empty((4, 2048, D), np.float32)
    for c, r in enumerate(res.results):
        n, h = c // 2, c % 2
        out[n, h * T:(h + 1) * T] = r["out"]
    return out


# version 06
# speedup vs baseline: 1.1033x; 1.0044x over previous
import numpy as np
from contextlib import ExitStack

import concourse.bass as bass
import concourse.mybir as mybir
from concourse.bass_utils import run_bass_kernel_spmd

F32 = mybir.dt.float32
BF16 = mybir.dt.bfloat16
I32 = mybir.dt.int32
U8 = mybir.dt.uint8
AF = mybir.ActivationFunctionType
ALU = mybir.AluOpType
AX = mybir.AxisListType

P = 128
T = 1024
THL = 1026
D = 4096
ND = 32
W3 = 3072
NW3 = 24
NWG = 12
WA = 1024
NWA = 8
NH = 16
DH = 64
DFF = 11008
NF = 86
FQ = [(0, 22), (22, 22), (44, 21), (65, 21)]
EPS = 1e-5
ENGINES = ("sync", "gpsimd", "tensor", "vector", "scalar")
ENG2SEM = {"sync": "sy", "gpsimd": "gq", "tensor": "tn",
           "vector": "vc", "scalar": "ac"}


class Builder:
    def __init__(self, nc, ctx):
        self.nc = nc
        self.ctx = ctx
        self.sched = {e: [] for e in ENGINES}
        self.sems = {}
        for name in ("sy", "gq", "tn", "vc", "ac"):
            self.new_sem(name)
        self.hiwater = {}

    def new_sem(self, name):
        h = self.ctx.enter_context(self.nc.semaphore(name))
        self.sems[name] = [h, 0]
        return name

    def emit(self, eng, fn):
        self.sched[eng].append(fn)

    def val(self, s):
        return self.sems[s][1]

    def wait(self, eng, s, v):
        if v is None or v <= 0:
            return
        if v <= self.hiwater.get((eng, s), 0):
            return
        self.hiwater[(eng, s)] = v
        h = self.sems[s][0]
        self.emit(eng, lambda q, h=h, v=v: q.wait_ge(h, v))

    def run(self, eng, fn, inc=None, n=1, fence=False):
        if fence and inc is None:
            inc = ENG2SEM[eng]
        if inc is None:
            self.emit(eng, fn)
            return None
        rec = self.sems[inc]
        rec[1] += n
        h = rec[0]
        self.emit(eng, lambda q, fn=fn, h=h, n=n: fn(q).then_inc(h, n))
        if fence:
            self.wait(eng, inc, rec[1])
        return rec[1]

    def qwait(self, eng, s):
        self.wait(eng, s, self.val(s))

    def dma(self, eng, out_ap, in_ap, sem=None, **kw):
        s = sem if sem else ("sy" if eng == "sync" else "gq")
        rec = self.sems[s]
        rec[1] += 16
        h = rec[0]
        self.emit(eng, lambda q, o=out_ap, i=in_ap, h=h, kw=kw:
                  q.dma_start(out=o, in_=i, **kw).then_inc(h, 16))
        return rec[1]

    def barrier(self):
        for eng, s in ENG2SEM.items():
            self.run(eng, lambda q: q.drain(), inc=s)
        snap = {s: self.val(s) for s in self.sems}
        for eng in ENGINES:
            for s, v in snap.items():
                if s != ENG2SEM[eng]:
                    self.wait(eng, s, v)

    def finalize(self):
        with self.nc.Block() as block:
            for eng in ENGINES:
                def body(q, eng=eng):
                    for f in self.sched[eng]:
                        f(q)
                getattr(block, eng)(body)


def build_nc(debug=False):
    nc = bass.Bass("TRN2", target_bir_lowering=False)

    x_ext = nc.dram_tensor("x_ext", [THL, D], F32, kind="ExternalInput").ap()
    conv_t = nc.dram_tensor("conv_t", [NWG, 3, ND, P, 256], F32, kind="ExternalInput").ap()
    conv_b24 = nc.dram_tensor("conv_b24", [NW3, P], F32, kind="ExternalInput").ap()
    wq2 = nc.dram_tensor("wq2", [WA, WA], F32, kind="ExternalInput").ap()
    wk2 = nc.dram_tensor("wk2", [WA, WA], F32, kind="ExternalInput").ap()
    wv2 = nc.dram_tensor("wv2", [WA, WA], F32, kind="ExternalInput").ap()
    wo2 = nc.dram_tensor("wo2", [WA, WA], F32, kind="ExternalInput").ap()
    bq8 = nc.dram_tensor("bq8", [8, P], F32, kind="ExternalInput").ap()
    bk8 = nc.dram_tensor("bk8", [8, P], F32, kind="ExternalInput").ap()
    bv_row = nc.dram_tensor("bv_row", [1, WA], F32, kind="ExternalInput").ap()
    bo8 = nc.dram_tensor("bo8", [8, P], F32, kind="ExternalInput").ap()
    projw = nc.dram_tensor("projw", [WA, D], F32, kind="ExternalInput").ap()
    pb_row = nc.dram_tensor("pb_row", [1, D], F32, kind="ExternalInput").ap()
    wg_t = nc.dram_tensor("wg_t", [NF, 8, P, 512], I32, kind="ExternalInput").ap()
    wu_t = nc.dram_tensor("wu_t", [NF, 8, P, 512], I32, kind="ExternalInput").ap()
    wd_t = nc.dram_tensor("wd_t", [NF, 8, P, 512], I32, kind="ExternalInput").ap()
    gs86 = nc.dram_tensor("gs86", [NF, P], F32, kind="ExternalInput").ap()
    us86 = nc.dram_tensor("us86", [NF, P], F32, kind="ExternalInput").ap()
    ds_row_d = nc.dram_tensor("ds_row", [1, D], F32, kind="ExternalInput").ap()
    out_d = nc.dram_tensor("out", [T, D], F32, kind="ExternalOutput").ap()
    spill_kind = "ExternalOutput" if debug else "Internal"
    spill = nc.dram_tensor("qkv_spill", [NW3, P, T], BF16, kind=spill_kind).ap()
    if debug:
        dbg_xnt = nc.dram_tensor("dbg_xnt", [ND, P, THL], BF16, kind="ExternalOutput").ap()
        dbg_h0 = nc.dram_tensor("dbg_h0", [22, P, T], BF16, kind="ExternalOutput").ap()
        dbg_heads = nc.dram_tensor("dbg_heads", [32, P, T], BF16, kind="ExternalOutput").ap()
        dbg_ow = nc.dram_tensor("dbg_ow", [8, P, T], BF16, kind="ExternalOutput").ap()
        dbg_att = nc.dram_tensor("dbg_att", [8, P, P], F32, kind="ExternalOutput").ap()
        dbg_q = nc.dram_tensor("dbg_q", [4, T, D], F32, kind="ExternalOutput").ap()

    with ExitStack() as ctx:
        b = Builder(nc, ctx)

        PS = ctx.enter_context(nc.psum_tensor("ps", [P, 4096], F32))
        PSB = PS.bitcast(BF16)

        def bankf(i, n=1):
            return PS[:, i * 512:(i + n) * 512]

        def bankb(i, n=1):
            return PSB[:, i * 1024:(i + n) * 1024]

        AR_BYTES = 212000
        arena = ctx.enter_context(nc.sbuf_tensor("arena", [P, AR_BYTES], U8))
        A32h = arena.bitcast(F32)
        ABFh = arena.bitcast(BF16)
        A32 = A32h.ap()
        ABF = ABFh.ap()

        class Alloc:
            cur = 0

            def take(self, nbytes, align=64):
                self.cur = (self.cur + align - 1) // align * align
                off = self.cur
                self.cur += nbytes
                assert self.cur <= AR_BYTES, f"SBUF arena overflow: {self.cur}"
                return off
        al = Alloc()

        def f32v(off, n, p0=0, pn=P):
            return A32[p0:p0 + pn, off // 4: off // 4 + n]

        def bf16v(off, n, p0=0, pn=P):
            return ABF[p0:p0 + pn, off // 2: off // 2 + n]

        o_identb = al.take(256)
        o_identf = al.take(512)
        o_mask = al.take(512)
        o_maskb = al.take(256)
        o_ones = al.take(512)
        o_rows = al.take(512)
        o_cb = al.take(128)
        o_sbq = al.take(64)
        o_bk = al.take(64)
        o_bo = al.take(64)
        o_gs = al.take(384)
        o_us = al.take(384)
        o_dsb = al.take(16384)
        o_pbb = al.take(16384)
        o_bvb = al.take(4096)
        o_eps = al.take(64)
        o_bn = al.take(256)
        o_agg = al.take(64)
        o_lnt = al.take(64)
        o_rstd = al.take(64)
        o_negmb = al.take(64)
        REG_A = al.take(65664)
        REG_B = al.take(49152)
        REG_C = al.take(49152)
        assert al.cur <= AR_BYTES

        identb = bf16v(o_identb, 128)
        maskb = bf16v(o_maskb, 128)
        identf = f32v(o_identf, 128)
        mask = f32v(o_mask, 128)
        ones1 = f32v(o_ones, 128, 0, 1)

        def xnt(d_):
            return bf16v(REG_A + d_ * 2052, THL)

        def xnt3d(d0, nd_, c0, ncols):
            base = REG_A // 2
            return ABF[:, base:base + 32 * THL] \
                .rearrange("p (c n) -> p c n", n=THL)[:, d0:d0 + nd_, c0:c0 + ncols]

        def qkvT(w):
            return bf16v(REG_B + w * 2048, T)

        def hT(i):
            return bf16v(REG_B + i * 2048, T)

        def col(off, j):
            return A32[0:P, off // 4 + j: off // 4 + j + 1]

        def g_(fn, inc=None, n=1, fence=False):
            return b.run("gpsimd", fn, inc=inc, n=n, fence=fence)

        g_(lambda q: q.memset(identb, 0.0), inc="gq", fence=True)
        g_(lambda q: q.affine_select(out=identb, in_=identb, compare_op=ALU.not_equal,
                                     fill=1.0, base=0, pattern=[[-1, P]], channel_multiplier=1))
        g_(lambda q: q.memset(identf, 0.0), inc="gq", fence=True)
        g_(lambda q: q.affine_select(out=identf, in_=identf, compare_op=ALU.not_equal,
                                     fill=1.0, base=0, pattern=[[-1, P]], channel_multiplier=1))
        g_(lambda q: q.memset(mask, 0.0), inc="gq", fence=True)
        g_(lambda q: q.affine_select(out=mask, in_=mask, compare_op=ALU.is_ge,
                                     fill=-1e9, base=0, pattern=[[-1, P]], channel_multiplier=1))
        g_(lambda q: q.memset(maskb, 0.0), inc="gq", fence=True)
        g_(lambda q: q.affine_select(out=maskb, in_=maskb, compare_op=ALU.is_ge,
                                     fill=-1e9, base=0, pattern=[[-1, P]], channel_multiplier=1))
        g_(lambda q: q.memset(ones1, 1.0))
        g_(lambda q: q.memset(f32v(o_eps, 1), EPS))
        setup_ms = g_(lambda q: q.drain(), inc="gq", n=1)

        rowbuf = lambda k: f32v(o_rows, 128, 0, k)

        def vec_to_cols(dram_ap, k, dest_off):
            t_ld = b.dma("sync", rowbuf(k), dram_ap)
            b.wait("tensor", "sy", t_ld)
            b.wait("tensor", "gq", setup_ms)
            ms = b.run("tensor", lambda q, k=k: q.transpose(
                PS[0:P, 0:k], rowbuf(k), identf[0:k, 0:k]), inc="tn")
            b.wait("vector", "tn", ms)
            b.run("vector", lambda q, k=k, d=dest_off: q.tensor_copy(
                A32[0:P, d // 4:d // 4 + k], PS[0:P, 0:k]), inc="vc")
            ev = b.run("vector", lambda q: q.drain(), inc="vc")
            b.wait("tensor", "vc", ev)
            b.wait("sync", "tn", ms)
            return ev

        vec_to_cols(conv_b24[:, :], NW3, o_cb)
        vec_to_cols(bq8[:, :], 8, o_sbq)
        vec_to_cols(bk8[:, :], 8, o_bk)
        vec_to_cols(bo8[:, :], 8, o_bo)
        vec_to_cols(gs86[:, :], NF, o_gs)
        ev_us = vec_to_cols(us86[:, :], NF, o_us)
        b.wait("scalar", "vc", ev_us)
        sbq_ms = b.run("scalar", lambda q: q.mul(
            A32[0:P, o_sbq // 4:o_sbq // 4 + 8],
            A32[0:P, o_sbq // 4:o_sbq // 4 + 8], 0.125), inc="ac")

        o_dsrow = REG_C
        o_pbrow = REG_C + 16384
        o_bvrow = REG_C + 32768
        b.new_sem("rw")
        t1 = b.dma("sync", f32v(o_dsrow, D, 0, 1), ds_row_d, sem="rw")
        t2 = b.dma("sync", f32v(o_pbrow, D, 0, 1), pb_row, sem="rw")
        t3 = b.dma("sync", f32v(o_bvrow, WA, 0, 1), bv_row, sem="rw")
        b.wait("tensor", "rw", t3)
        for src_off, dst_off, n in ((o_dsrow, o_dsb, D), (o_pbrow, o_pbb, D),
                                    (o_bvrow, o_bvb, WA)):
            for j in range(n // 512):
                ms = b.run("tensor", lambda q, s=src_off, j=j: q.matmul(
                    bankf(1), ones1, A32[0:1, s // 4 + j * 512: s // 4 + (j + 1) * 512],
                    start=True, stop=True), inc="tn")
                b.wait("vector", "tn", ms)
                b.run("vector", lambda q, d=dst_off, j=j: q.tensor_copy(
                    A32[0:P, d // 4 + j * 512: d // 4 + (j + 1) * 512], bankf(1)), inc="vc")
                ev = b.run("vector", lambda q: q.drain(), inc="vc")
                b.wait("tensor", "vc", ev)

        b.barrier()

        o_xring = REG_B
        o_xnring = REG_B + 32768
        bn = f32v(o_bn, 48)
        agg = f32v(o_agg, 2)
        b.new_sem("xr0"); b.new_sem("xr1")
        xn_ms = {}
        trp_ms = {}
        ln_ld = {}
        ev_ms = {}
        for t_ in range(9):
            rt = 128 if t_ < 8 else 2
            xv = f32v(o_xring + (t_ % 2) * 16384, D, 0, rt)
            if t_ >= 2:
                b.wait("sync", "vc", xn_ms[t_ - 2])
            ln_ld[t_] = b.dma("sync", xv, x_ext[t_ * P: t_ * P + rt, :],
                              sem=f"xr{t_ % 2}")
            b.wait("vector", f"xr{t_ % 2}", ln_ld[t_])
            for i in range(8):
                b.run("vector", lambda q, xv=xv, i=i, rt=rt: q.bn_stats(
                    bn[0:rt, 6 * i:6 * i + 6], xv[0:rt, 512 * i:512 * (i + 1)]),
                    inc="vc" if i == 7 else None, fence=(i == 7))
            st_ms = b.run("vector", lambda q, rt=rt: q.bn_aggr(
                agg[0:rt, 0:2], bn[0:rt, 0:48]), inc="vc")
            b.wait("scalar", "vc", st_ms)
            b.run("scalar", lambda q, rt=rt: q.activation(
                f32v(o_lnt, 1, 0, rt), agg[0:rt, 1:2], AF.Ln,
                bias=f32v(o_eps, 1, 0, rt), scale=1.0), fence=True)
            rs_ms = b.run("scalar", lambda q, rt=rt: q.activation(
                f32v(o_rstd, 1, 0, rt), f32v(o_lnt, 1, 0, rt), AF.Exp,
                bias=0.0, scale=-0.5), inc="ac")
            b.wait("vector", "ac", rs_ms)
            nm_ms = b.run("vector", lambda q, rt=rt: q.scalar_tensor_tensor(
                f32v(o_negmb, 1, 0, rt), agg[0:rt, 0:1], -1.0,
                f32v(o_rstd, 1, 0, rt), op0=ALU.mult, op1=ALU.mult),
                inc="vc", fence=True)
            if t_ >= 2:
                b.wait("vector", "tn", trp_ms[t_ - 2])
            xnv = bf16v(o_xnring + (t_ % 2) * 8192, D, 0, rt)
            b.run("vector", lambda q, xv=xv, xnv=xnv, rt=rt: q.tensor_scalar(
                xnv, xv, f32v(o_rstd, 1, 0, rt), f32v(o_negmb, 1, 0, rt),
                op0=ALU.mult, op1=ALU.add), inc="vc", fence=True)
            b.run("vector", lambda q, xv=xv, rt=rt: q.tensor_tensor(
                xv, xv, f32v(o_pbb, D, 0, rt), op=ALU.add), inc="vc", fence=True)
            xn_ms[t_] = b.run("vector", lambda q: q.drain(), inc="vc")
            b.wait("sync", "vc", xn_ms[t_])
            if t_ == 0:
                b.dma("sync", out_d[0:127, :], f32v(o_xring + 0, D, 1, 127))
            elif t_ < 8:
                b.dma("sync", out_d[t_ * P - 1: (t_ + 1) * P - 1, :], xv)
            else:
                b.dma("sync", out_d[T - 1: T, :], f32v(o_xring + (t_ % 2) * 16384, D, 0, 1))
            b.wait("tensor", "vc", xn_ms[t_])
            for g in range(8):
                gg = t_ * 8 + g
                bk_ = 2 + gg % 2
                if gg >= 2:
                    b.wait("tensor", "vc", ev_ms[gg - 2])
                for j in range(4):
                    d_ = g * 4 + j
                    last = (j == 3)
                    m = b.run("tensor", lambda q, bk_=bk_, j=j, d_=d_, xnv=xnv, rt=rt: q.transpose(
                        bankb(bk_)[:, j * 128:j * 128 + rt],
                        xnv[0:rt, d_ * 128:(d_ + 1) * 128], identb[0:rt, 0:rt]),
                        inc="tn" if last else None)
                    if last:
                        grp_ms = m
                if g == 7:
                    trp_ms[t_] = grp_ms
                b.wait("vector", "tn", grp_ms)
                src3 = bankb(bk_).rearrange("p (c n) -> p c n", n=128)[:, 0:4, 0:rt]
                ev_ms[gg] = b.run("vector", lambda q, g=g, t_=t_, rt=rt, src3=src3:
                                  q.tensor_copy(xnt3d(g * 4, 4, t_ * 128, rt), src3),
                                  inc="vc")
        b.barrier()
        if debug:
            for d_ in range(ND):
                b.dma("sync", dbg_xnt[d_, :, :], xnt(d_))
            b.barrier()

        o_cring = REG_C
        CR = 12
        for s_ in range(CR):
            b.new_sem(f"cr{s_}")
        cons = []
        evac_ms = {}
        for wg in range(NWG):
            base = 0 if wg % 2 == 0 else 4
            if wg >= 2:
                b.wait("tensor", "ac", evac_ms[wg - 2])
            for tap in range(3):
                for d_ in range(ND):
                    gi = wg * 96 + tap * 32 + d_
                    slot = gi % CR
                    if gi >= CR:
                        b.wait("gpsimd", "tn", cons[gi - CR])
                    ld = b.dma("gpsimd", bf16v(o_cring + slot * 512, 256),
                               conv_t[wg, tap, d_, :, :], sem=f"cr{slot}")
                    b.wait("tensor", f"cr{slot}", ld)
                    first = (tap == 0 and d_ == 0)
                    lastk = (tap == 2 and d_ == ND - 1)
                    for wt in range(2):
                        for c in range(2):
                            bk_ = base + wt * 2 + c
                            is_last_of_tile = (wt == 1 and c == 1)
                            b.run("tensor", lambda q, bk_=bk_, slot=slot, wt=wt, c=c,
                                  d_=d_, tap=tap, first=first, lastk=lastk: q.matmul(
                                bankf(bk_),
                                bf16v(o_cring + slot * 512, 256)[:, wt * 128:(wt + 1) * 128],
                                xnt(d_)[:, c * 512 + tap: c * 512 + tap + 512],
                                start=first, stop=lastk),
                                inc="tn" if is_last_of_tile else None)
                    cons.append(b.val("tn"))
            stop_all = cons[-1]
            b.wait("scalar", "tn", stop_all)
            for wt in range(2):
                for c in range(2):
                    w_ = wg * 2 + wt
                    e = b.run("scalar", lambda q, base=base, wt=wt, c=c, w_=w_: q.activation(
                        qkvT(w_)[:, c * 512:(c + 1) * 512], bankf(base + wt * 2 + c),
                        AF.Identity, bias=col(o_cb, w_), scale=1.0), inc="ac")
            evac_ms[wg] = e
            e = b.run("scalar", lambda q: q.drain(), inc="ac")
            b.wait("sync", "ac", e)
            for wt in range(2):
                w_ = wg * 2 + wt
                b.dma("sync", spill[w_, :, :], qkvT(w_))
        b.barrier()

        o_sil = REG_C
        o_gring = REG_C + 8192
        o_uring = REG_C + 19456
        o_wdring = REG_C + 30720
        o_ftmp = REG_C + 36864
        o_pin = REG_C + 40960
        GR = 11
        WR = 6
        for s_ in range(GR):
            b.new_sem(f"gr{s_}")
            b.new_sem(f"ur{s_}")
        for s_ in range(WR):
            b.new_sem(f"wr{s_}")
        gslotv = lambda s: bf16v(o_gring + s * 1024, 512)
        uslotv = lambda s: bf16v(o_uring + s * 1024, 512)
        wdslotv = lambda s: bf16v(o_wdring + s * 1024, 512)
        ftmpv = lambda s: f32v(o_ftmp + s * 2048, 512)
        pinv = lambda s: f32v(o_pin + s * 2048, 512)
        eslotv = lambda s: f32v(o_gring + s * 2048, 512)
        eslot_rel = {}

        for s_ in range(4):
            b.new_sem(f"pn{s_}")
        gu_cons = []
        gu_loads = []
        flush_ms = {}
        for qf, (f0, nf_) in enumerate(FQ):
            sil_ms = {}
            h_ms = {}
            for fi in range(nf_):
                ft = f0 + fi
                par = fi % 2
                pgate = PS[:, (par * 2048):(par * 2048 + 1024)]
                pup = PS[:, (par * 2048 + 1024):(par * 2048 + 2048)]
                if fi >= 2:
                    b.wait("tensor", "ac", sil_ms[fi - 2])
                    b.wait("tensor", "vc", h_ms[fi - 2])
                ldg = [None] * 8
                ldu = [None] * 8
                for dg in range(8):
                    gi = len(gu_loads)
                    if gi >= 2 * GR:
                        b.wait("gpsimd", "tn", gu_cons[gi - 2 * GR])
                    sl = (fi * 8 + dg) % GR
                    ldg[dg] = (sl, b.dma("gpsimd", gslotv(sl), wg_t[ft, dg, :, :],
                                          sem=f"gr{sl}"))
                    gu_loads.append(ldg[dg]); gu_cons.append(None)
                    gi = len(gu_loads)
                    if gi >= 2 * GR:
                        b.wait("gpsimd", "tn", gu_cons[gi - 2 * GR])
                    ldu[dg] = (sl, b.dma("gpsimd", uslotv(sl), wu_t[ft, dg, :, :],
                                          sem=f"ur{sl}"))
                    gu_loads.append(ldu[dg]); gu_cons.append(None)
                for d_ in range(ND):
                    dg, j = d_ // 4, d_ % 4
                    sl = (fi * 8 + dg) % GR
                    b.wait("tensor", f"gr{sl}", ldg[dg][1])
                    b.wait("tensor", f"ur{sl}", ldu[dg][1])
                    first = (d_ == 0)
                    lastk = (d_ == ND - 1)
                    for mat, psum, ring in ((0, pgate, gslotv(sl)), (1, pup, uslotv(sl))):
                        for c in range(2):
                            do_inc = (mat == 1 and c == 1 and j == 3)
                            b.run("tensor", lambda q, psum=psum, ring=ring, j=j, c=c,
                                  d_=d_, first=first, lastk=lastk: q.matmul(
                                psum[:, c * 512:(c + 1) * 512],
                                ring[:, j * 128:(j + 1) * 128],
                                xnt(d_)[:, 1 + c * 512: 1 + c * 512 + 512],
                                start=first, stop=lastk),
                                inc="tn" if do_inc else None)
                    if j == 3:
                        v = b.val("tn")
                        gu_cons[len(gu_loads) - 16 + dg * 2] = v
                        gu_cons[len(gu_loads) - 16 + dg * 2 + 1] = v
                stop_ms = b.val("tn")
                b.wait("scalar", "tn", stop_ms)
                if fi >= 2:
                    b.wait("scalar", "vc", h_ms[fi - 2])
                sil_ms[fi] = b.run("scalar", lambda q, par=par, pgate=pgate, ft=ft: q.activation(
                    f32v(o_sil + par * 4096, 1024), pgate, AF.Silu,
                    bias=0.0, scale=col(o_gs, ft)), inc="ac")
                b.wait("vector", "ac", sil_ms[fi])
                b.wait("vector", "tn", stop_ms)
                h_ms[fi] = b.run("vector", lambda q, fi=fi, par=par, pup=pup, ft=ft: q.scalar_tensor_tensor(
                    hT(fi), pup, col(o_us, ft), f32v(o_sil + par * 4096, 1024),
                    op0=ALU.mult, op1=ALU.mult), inc="vc")
            last_sil = sil_ms[nf_ - 1]
            last_h = h_ms[nf_ - 1]
            wd_cons = []
            wd_loads = []
            b.wait("tensor", "ac", last_sil)
            b.wait("tensor", "vc", last_h)
            for dt in range(8):
                bank_stop = [None] * 8
                pin_ld = {ts: b.dma("sync", pinv(ts % 4), out_d[
                    ts * P:(ts + 1) * P, dt * 512:(dt + 1) * 512],
                    sem=f"pn{ts % 4}")
                    for ts in range(4)}
                for fi in range(nf_):
                    ft = f0 + fi
                    gi = len(wd_loads)
                    if gi >= WR:
                        b.wait("gpsimd", "tn", wd_cons[gi - WR])
                    sl = gi % WR
                    ld = b.dma("gpsimd", wdslotv(sl), wd_t[ft, dt, :, :],
                               sem=f"wr{sl}")
                    wd_loads.append(ld)
                    b.wait("tensor", f"wr{sl}", ld)
                    for ts in range(8):
                        if fi == 0 and dt > 0:
                            b.wait("tensor", *flush_ms[(dt - 1, ts)])
                        do_inc = (ts == 7) or (fi == nf_ - 1)
                        m = b.run("tensor", lambda q, ts=ts, fi=fi, sl=sl, nf_=nf_: q.matmul(
                            bankf(ts), hT(fi)[:, ts * 128:(ts + 1) * 128], wdslotv(sl),
                            start=(fi == 0), stop=(fi == nf_ - 1)),
                            inc="tn" if do_inc else None)
                        if fi == nf_ - 1:
                            bank_stop[ts] = b.val("tn")
                    wd_cons.append(b.val("tn"))
                for ts in range(8):
                    eng = "scalar" if ts % 2 else "vector"
                    sem = "ac" if ts % 2 else "vc"
                    b.wait(eng, "tn", bank_stop[ts])
                    b.wait(eng, "vc", eslot_rel.get(ts, 0))
                    m = b.run(eng, lambda q, ts=ts, eng=eng:
                              (q.copy if eng == "scalar" else q.tensor_copy)(
                                  eslotv(ts), bankf(ts)), inc=sem, fence=True)
                    flush_ms[(dt, ts)] = (sem, m)
                for ts in range(8):
                    sl = ts % 4
                    if ts >= 4:
                        pin_ld[ts] = b.dma("sync", pinv(sl), out_d[
                            ts * P:(ts + 1) * P, dt * 512:(dt + 1) * 512],
                            sem=f"pn{sl}")
                    sem, m = flush_ms[(dt, ts)]
                    b.wait("vector", sem, m)
                    b.wait("vector", f"pn{sl}", pin_ld[ts])
                    eslot_rel[ts] = b.run("vector", lambda q, ts=ts, dt=dt: q.tensor_tensor(
                        ftmpv(ts % 2), eslotv(ts),
                        A32[0:P, o_dsb // 4 + dt * 512: o_dsb // 4 + (dt + 1) * 512],
                        op=ALU.mult), inc="vc", fence=True)
                    fm = b.run("vector", lambda q, ts=ts, sl=sl: q.tensor_tensor(
                        pinv(sl), ftmpv(ts % 2), pinv(sl), op=ALU.add), inc="vc")
                    b.wait("sync", "vc", fm)
                    b.dma("sync", out_d[ts * P:(ts + 1) * P, dt * 512:(dt + 1) * 512],
                          pinv(sl))
                    if debug:
                        b.dma("sync", dbg_q[qf, ts * P:(ts + 1) * P,
                                            dt * 512:(dt + 1) * 512], pinv(sl))
            if debug and qf == 0:
                b.wait("sync", "vc", last_h)
                for fi in range(nf_):
                    b.dma("sync", dbg_h0[fi, :, :], hT(fi))
            b.barrier()
            flush_ms = {}

        qhT = lambda ht: bf16v(REG_A + ht * 2048, T)
        khT = lambda ht: bf16v(REG_A + 16384 + ht * 2048, T)
        vh = lambda ts: bf16v(REG_A + 32768 + ts * 2048, T)
        oT = lambda ht: bf16v(REG_A + 49152 + ht * 2048, T)
        owT = lambda wt: bf16v(REG_C + wt * 2048, T)
        hw = lambda wc: bf16v(REG_C + 16384 + wc * 2048, T)
        o_pring = REG_C + 32768
        o_pin3 = REG_C + 35840
        o_sm = REG_C + 41984
        o_pf = o_sm + 2048
        o_pbf = o_pf + 2048
        o_ptb = o_pbf + 1024
        o_nm = o_ptb + 512
        o_ssum = o_nm + 128
        o_rcp = o_ssum + 128
        pring = lambda s: bf16v(o_pring + s * 1024, 512)
        pin3 = lambda s: f32v(o_pin3 + s * 2048, 512)
        for s_ in range(3):
            b.new_sem(f"pr{s_}")
            b.new_sem(f"pq{s_}")
        for s_ in range(NWA):
            b.new_sem(f"hw{s_}")
        b.new_sem("qk")

        for w in range(NW3):
            b.dma("sync", qkvT(w), spill[w, :, :], sem="qk")
        qk_all = b.val("qk")

        hb_prev = [None, None]
        last_mm = [0]

        def head_proj(wsrc, kind):
            b.wait("gpsimd", "tn", last_mm[0])
            hld = [b.dma("gpsimd", hw(wc), wsrc[wc * P:(wc + 1) * P, :],
                         sem=f"hw{wc}") for wc in range(NWA)]
            ev = {}
            if kind == "v":
                for ts in range(8):
                    par = ts % 2
                    ps = PS[:, par * 2048: par * 2048 + 1024]
                    if ts >= 2:
                        b.wait("tensor", "vc", ev[ts - 2])
                    elif hb_prev[par] is not None:
                        b.wait("tensor", *hb_prev[par])
                    for wc in range(NWA):
                        b.wait("tensor", f"hw{wc}", hld[wc])
                        b.wait("tensor", "qk", qk_all)
                        for c in range(2):
                            do_inc = (wc == NWA - 1 and c == 1)
                            b.run("tensor", lambda q, ps=ps, wc=wc, c=c, ts=ts: q.matmul(
                                ps[:, c * 512:(c + 1) * 512],
                                qkvT(16 + wc)[:, ts * 128:(ts + 1) * 128],
                                hw(wc)[:, c * 512:(c + 1) * 512],
                                start=(wc == 0), stop=(wc == NWA - 1)),
                                inc="tn" if do_inc else None)
                    sm_ = b.val("tn")
                    b.wait("vector", "tn", sm_)
                    ev[ts] = b.run("vector", lambda q, ts=ts, ps=ps: q.tensor_tensor(
                        vh(ts), ps, A32[0:P, o_bvb // 4: o_bvb // 4 + 1024],
                        op=ALU.add), inc="vc")
                hb_prev[0] = ("vc", ev[6])
                hb_prev[1] = ("vc", ev[7])
                last_mm[0] = b.val("tn")
                return ev[7]
            else:
                src0 = 0 if kind == "q" else 8
                dest = qhT if kind == "q" else khT
                bias_off = o_sbq if kind == "q" else o_bk
                scale = 0.125 if kind == "q" else 1.0
                for ht in range(NWA):
                    par = ht % 2
                    ps = PS[:, par * 2048: par * 2048 + 1024]
                    if ht >= 2:
                        b.wait("tensor", "ac", ev[ht - 2])
                    elif hb_prev[par] is not None:
                        b.wait("tensor", *hb_prev[par])
                    for wc in range(NWA):
                        b.wait("tensor", f"hw{wc}", hld[wc])
                        b.wait("tensor", "qk", qk_all)
                        for c in range(2):
                            do_inc = (wc == NWA - 1 and c == 1)
                            b.run("tensor", lambda q, ps=ps, wc=wc, c=c, ht=ht, src0=src0: q.matmul(
                                ps[:, c * 512:(c + 1) * 512],
                                hw(wc)[:, ht * 128:(ht + 1) * 128],
                                qkvT(src0 + wc)[:, c * 512:(c + 1) * 512],
                                start=(wc == 0), stop=(wc == NWA - 1)),
                                inc="tn" if do_inc else None)
                    sm_ = b.val("tn")
                    b.wait("scalar", "tn", sm_)
                    ev[ht] = b.run("scalar", lambda q, ht=ht, ps=ps, dest=dest,
                                   bias_off=bias_off, scale=scale: q.activation(
                        dest(ht), ps, AF.Identity, bias=col(bias_off, ht), scale=scale),
                        inc="ac")
                hb_prev[0] = ("ac", ev[6])
                hb_prev[1] = ("ac", ev[7])
                last_mm[0] = b.val("tn")
                return ev[7]

        head_proj(wq2, "q")
        head_proj(wk2, "k")
        head_proj(wv2, "v")
        b.barrier()
        if debug:
            for j in range(8):
                b.dma("sync", dbg_heads[j, :, :], qhT(j))
                b.dma("sync", dbg_heads[8 + j, :, :], khT(j))
                b.dma("sync", dbg_heads[16 + j, :, :], vh(j))
            b.barrier()

        sm_ms = {}
        exp_ms = {}
        rcp_ms = {}
        pbf_ms = {}
        pt_ms = {}
        ptev_ms = {}
        pv_ms = {}
        oev_ms = {}
        sum_ms4 = {}
        bankrel = {}
        for blk in range(8):
            for h in range(NH):
                i = blk * NH + h
                par = i % 4
                ip = i // 2
                par2 = ip % 2
                ht = h // 2
                hrow = (h % 2) * 64
                ps_s = PS[:, par * 512: par * 512 + 128]
                if i >= 4:
                    b.wait("tensor", bankrel[i - 4][0], bankrel[i - 4][1])
                b.run("tensor", lambda q, ps_s=ps_s: q.matmul(
                    ps_s, identb, maskb, start=True, stop=False))
                b.run("tensor", lambda q, ps_s=ps_s, ht=ht, hrow=hrow, blk=blk: q.matmul(
                    ps_s,
                    qhT(ht)[hrow:hrow + 64, blk * 128:(blk + 1) * 128],
                    khT(ht)[hrow:hrow + 64, blk * 128:(blk + 1) * 128],
                    start=False, stop=True), inc="tn")
                s_ms = b.val("tn")
                b.wait("vector", "tn", s_ms)
                if i >= 4:
                    b.wait("vector", "ac", exp_ms[i - 4])
                sm_ms[i] = b.run("vector", lambda q, ps_s=ps_s, par=par: q.reduce_max(
                    f32v(o_nm + par * 32, 1), ps_s, axis=AX.X, negate=True), inc="vc")
                b.wait("scalar", "vc", sm_ms[i])
                if i >= 4:
                    b.wait("scalar", "vc", pbf_ms[i - 4])
                pfv = f32v(o_pf + par * 512, 128)
                exp_ms[i] = b.run("scalar", lambda q, pfv=pfv, ps_s=ps_s, par=par: q.activation(
                    pfv, ps_s, AF.Exp, bias=f32v(o_nm + par * 32, 1), scale=1.0), inc="ac")
                bankrel[i] = ("ac", exp_ms[i])
                b.wait("vector", "ac", exp_ms[i])
                if i >= 4:
                    b.wait("vector", "ac", rcp_ms[i - 4])
                sum_ms4[i] = b.run("vector", lambda q, pfv=pfv, par=par: q.reduce_sum(
                    f32v(o_ssum + par * 32, 1), pfv, axis=AX.X), inc="vc")
                b.wait("scalar", "vc", sum_ms4[i])
                b.run("scalar", lambda q, par=par: q.activation(
                    f32v(o_rcp + par * 32, 1), f32v(o_ssum + par * 32, 1),
                    AF.Ln, bias=0.0, scale=1.0), fence=True)
                rcp_ms[i] = b.run("scalar", lambda q, par=par: q.activation(
                    f32v(o_rcp + par * 32, 1), f32v(o_rcp + par * 32, 1),
                    AF.Exp, bias=0.0, scale=-1.0), inc="ac", fence=True)
                b.wait("vector", "ac", rcp_ms[i])
                pbfv = bf16v(o_pbf + par * 256, 128)
                if i >= 4:
                    b.wait("vector", "tn", pt_ms[i - 4])
                pbf_ms[i] = b.run("vector", lambda q, pbfv=pbfv, pfv=pfv, par=par: q.tensor_scalar(
                    pbfv, pfv, f32v(o_rcp + par * 32, 1), None, op0=ALU.mult), inc="vc")
                b.wait("tensor", "vc", pbf_ms[i])
                if i >= 2:
                    b.wait("tensor", "vc", ptev_ms[i - 2])
                b.run("tensor", lambda q, i=i, pbfv=pbfv: q.transpose(
                    bankb(4 + i % 2)[:, 0:128], pbfv, identb), inc="tn")
                pt_ms[i] = b.val("tn")
                b.wait("vector", "tn", pt_ms[i])
                ptbv = bf16v(o_ptb + (i % 2) * 256, 128)
                if i >= 2:
                    b.wait("vector", "tn", pv_ms[i - 2])
                ptev_ms[i] = b.run("vector", lambda q, ptbv=ptbv, i=i: q.tensor_copy(
                    ptbv, bankb(4 + i % 2)[:, 0:128]), inc="vc")
                if debug and i < 2:
                    dsl = i * 4
                    b.wait("vector", "ac", exp_ms[i])
                    c1 = b.run("vector", lambda q, dsl=dsl, ps_s=ps_s: q.tensor_copy(
                        f32v(o_sil + 2048 * (dsl % 2), 128), ps_s), inc="vc")
                    bankrel[i] = ("vc", c1)
                    b.wait("sync", "vc", c1)
                    b.dma("sync", dbg_att[dsl, :, :], f32v(o_sil + 2048 * (dsl % 2), 128))
                    b.dma("sync", dbg_att[dsl + 1, :, :], f32v(o_sil + 2048 * (dsl % 2), 128))
                    b.dma("sync", dbg_att[dsl + 2, :, :], pfv)
                    b.wait("sync", "tn", pt_ms[i])
                    c2 = b.run("vector", lambda q, dsl=dsl, i=i: q.tensor_copy(
                        f32v(o_sil + 2048 * (dsl % 2), 128), bankb(4 + i % 2)[:, 0:128]), inc="vc")
                    ptev_ms[i] = c2
                    b.wait("sync", "vc", c2)
                    b.dma("sync", dbg_att[dsl + 3, :, :], f32v(o_sil + 2048 * (dsl % 2), 128))
                b.wait("tensor", "vc", ptev_ms[i])
                if h % 2 == 0 and ip >= 2:
                    b.wait("tensor", "vc", oev_ms[ip - 2])
                b.run("tensor", lambda q, hrow=hrow, par2=par2, blk=blk, h=h, ptbv=ptbv: q.matmul(
                    PS[hrow:hrow + 64, (6 + par2) * 512: (6 + par2) * 512 + 128],
                    vh(blk)[:, h * 64:(h + 1) * 64], ptbv,
                    start=True, stop=True), inc="tn")
                pv_ms[i] = b.val("tn")
                if h % 2 == 1:
                    op_ms = pv_ms[i]
                    b.wait("vector", "tn", op_ms)
                    oev_ms[ip] = b.run("vector", lambda q, ht=ht, blk=blk, par2=par2: q.tensor_copy(
                        oT(ht)[:, blk * 128:(blk + 1) * 128],
                        PS[:, (6 + par2) * 512: (6 + par2) * 512 + 128]), inc="vc")
        b.barrier()
        if debug:
            for j in range(8):
                b.dma("sync", dbg_heads[24 + j, :, :], oT(j))
            b.barrier()

        hld = [b.dma("gpsimd", hw(wc), wo2[wc * P:(wc + 1) * P, :],
                     sem=f"hw{wc}") for wc in range(NWA)]
        evo = {}
        for wt in range(NWA):
            par = wt % 2
            ps = PS[:, par * 2048: par * 2048 + 1024]
            if wt >= 2:
                b.wait("tensor", "ac", evo[wt - 2])
            for wc in range(NWA):
                b.wait("tensor", f"hw{wc}", hld[wc])
                for c in range(2):
                    do_inc = (wc == NWA - 1 and c == 1)
                    b.run("tensor", lambda q, ps=ps, wc=wc, c=c, wt=wt: q.matmul(
                        ps[:, c * 512:(c + 1) * 512],
                        hw(wc)[:, wt * 128:(wt + 1) * 128],
                        oT(wc)[:, c * 512:(c + 1) * 512],
                        start=(wc == 0), stop=(wc == NWA - 1)),
                        inc="tn" if do_inc else None)
            sm_ = b.val("tn")
            b.wait("scalar", "tn", sm_)
            evo[wt] = b.run("scalar", lambda q, wt=wt, ps=ps: q.activation(
                owT(wt), ps, AF.Identity, bias=col(o_bo, wt), scale=1.0), inc="ac")
        b.barrier()
        if debug:
            for j in range(8):
                b.dma("sync", dbg_ow[j, :, :], owT(j))
            b.barrier()

        pw_cons = []
        pw_loads = []
        fin_ms = {}
        ptmp_st = {}
        for dt in range(8):
            bank_stop = [None] * 8
            pin_ld = {ts: b.dma("sync", pin3(ts % 3), out_d[
                ts * P:(ts + 1) * P, dt * 512:(dt + 1) * 512],
                sem=f"pq{ts % 3}")
                for ts in range(3)}
            for wc in range(NWA):
                gi = len(pw_loads)
                if gi >= 3:
                    b.wait("gpsimd", "tn", pw_cons[gi - 3])
                sl = gi % 3
                ld = b.dma("gpsimd", pring(sl), projw[wc * P:(wc + 1) * P, dt * 512:(dt + 1) * 512],
                           sem=f"pr{sl}")
                pw_loads.append(ld)
                b.wait("tensor", f"pr{sl}", ld)
                for ts in range(8):
                    if wc == 0 and dt > 0:
                        b.wait("tensor", "vc", fin_ms[(dt - 1, ts)])
                    do_inc = (ts == 7) or (wc == NWA - 1)
                    b.run("tensor", lambda q, ts=ts, wc=wc, sl=sl: q.matmul(
                        bankf(ts), owT(wc)[:, ts * 128:(ts + 1) * 128], pring(sl),
                        start=(wc == 0), stop=(wc == NWA - 1)),
                        inc="tn" if do_inc else None)
                    if wc == NWA - 1:
                        bank_stop[ts] = b.val("tn")
                pw_cons.append(b.val("tn"))
            for ts in range(8):
                sl = ts % 3
                if ts >= 3:
                    pin_ld[ts] = b.dma("sync", pin3(sl), out_d[
                        ts * P:(ts + 1) * P, dt * 512:(dt + 1) * 512],
                        sem=f"pq{sl}")
                b.wait("vector", "tn", bank_stop[ts])
                b.wait("vector", f"pq{sl}", pin_ld[ts])
                fin_ms[(dt, ts)] = b.run("vector", lambda q, ts=ts, sl=sl: q.tensor_tensor(
                    pin3(sl), bankf(ts), pin3(sl), op=ALU.add), inc="vc")
                b.wait("sync", "vc", fin_ms[(dt, ts)])
                b.dma("sync", out_d[ts * P:(ts + 1) * P, dt * 512:(dt + 1) * 512],
                      pin3(sl))
        b.wait("sync", "gq", b.val("gq"))
        b.wait("sync", "sy", b.val("sy"))

        b.finalize()
    return nc


def _prep_shared(inputs):
    a = {k: np.asarray(v) for k, v in inputs.items()}
    conv_w = a["L0_attn_conv_w"].astype(np.float32, copy=False)
    conv_t = np.ascontiguousarray(
        conv_w.reshape(3, ND, P, NWG, 256).transpose(3, 0, 1, 2, 4))
    gw = a["L0_ffn_gate_w"]
    uw = a["L0_ffn_up_w"]
    dw = a["L0_ffn_down_w"]
    wgT = np.ascontiguousarray(gw.T)
    wuT = np.ascontiguousarray(uw.T)
    wdT = np.ascontiguousarray(dw.T)
    wg_t = np.ascontiguousarray(
        wgT.reshape(ND, P, NF, P).transpose(2, 0, 1, 3)
           .reshape(NF, 8, 4, P, P).transpose(0, 1, 3, 2, 4).reshape(NF, 8, P, 512))
    wu_t = np.ascontiguousarray(
        wuT.reshape(ND, P, NF, P).transpose(2, 0, 1, 3)
           .reshape(NF, 8, 4, P, P).transpose(0, 1, 3, 2, 4).reshape(NF, 8, P, 512))
    wd_t = np.ascontiguousarray(
        wdT.reshape(NF, P, 8, 512).transpose(0, 2, 1, 3))
    shared = {
        "conv_t": conv_t,
        "conv_b24": np.ascontiguousarray(a["L0_attn_conv_b"].reshape(NW3, P).astype(np.float32, copy=False)),
        "wq2": np.ascontiguousarray(a["L0_attn_wq"].reshape(WA, WA).astype(np.float32, copy=False)),
        "wk2": np.ascontiguousarray(a["L0_attn_wk"].reshape(WA, WA).astype(np.float32, copy=False)),
        "wv2": np.ascontiguousarray(a["L0_attn_wv"].reshape(WA, WA).astype(np.float32, copy=False)),
        "wo2": np.ascontiguousarray(a["L0_attn_wo"].reshape(WA, WA).astype(np.float32, copy=False)),
        "bq8": np.ascontiguousarray(a["L0_attn_bq"].reshape(8, P).astype(np.float32, copy=False)),
        "bk8": np.ascontiguousarray(a["L0_attn_bk"].reshape(8, P).astype(np.float32, copy=False)),
        "bv_row": np.ascontiguousarray(a["L0_attn_bv"].reshape(1, WA).astype(np.float32, copy=False)),
        "bo8": np.ascontiguousarray(a["L0_attn_bo"].reshape(8, P).astype(np.float32, copy=False)),
        "projw": np.ascontiguousarray(a["L0_attn_proj_w"].astype(np.float32, copy=False)),
        "pb_row": np.ascontiguousarray(a["L0_attn_proj_b"].reshape(1, D).astype(np.float32, copy=False)),
        "wg_t": wg_t, "wu_t": wu_t, "wd_t": wd_t,
        "gs86": np.ascontiguousarray(a["L0_ffn_gate_s"].reshape(NF, P).astype(np.float32, copy=False)),
        "us86": np.ascontiguousarray(a["L0_ffn_up_s"].reshape(NF, P).astype(np.float32, copy=False)),
        "ds_row": np.ascontiguousarray(a["L0_ffn_down_s"].reshape(1, D).astype(np.float32, copy=False)),
    }
    return shared, a


def _shard_x(x, c):
    n, h = c // 2, c % 2
    xe = np.zeros((THL, D), np.float32)
    if h == 0:
        xe[1:THL] = x[n, 0:THL - 1]
    else:
        xe[0:THL - 1] = x[n, T - 1:2 * T]
    return xe


def kernel(**inputs):
    shared, a = _prep_shared(inputs)
    x = a["L0_attn_inputs"].astype(np.float32, copy=False)
    nc = build_nc()
    in_maps = []
    for c in range(8):
        m = dict(shared)
        m["x_ext"] = _shard_x(x, c)
        in_maps.append(m)
    res = run_bass_kernel_spmd(nc, in_maps, core_ids=list(range(8)))
    out = np.empty((4, 2048, D), np.float32)
    for c, r in enumerate(res.results):
        n, h = c // 2, c % 2
        out[n, h * T:(h + 1) * T] = r["out"]
    return out


# version 07
# speedup vs baseline: 1.1404x; 1.0337x over previous
import numpy as np
from contextlib import ExitStack

import concourse.bass as bass
import concourse.mybir as mybir
from concourse.bass_utils import run_bass_kernel_spmd

F32 = mybir.dt.float32
BF16 = mybir.dt.bfloat16
I32 = mybir.dt.int32
U8 = mybir.dt.uint8
AF = mybir.ActivationFunctionType
ALU = mybir.AluOpType
AX = mybir.AxisListType

P = 128
T = 1024
THL = 1026
D = 4096
ND = 32
W3 = 3072
NW3 = 24
NWG = 12
WA = 1024
NWA = 8
NH = 16
DH = 64
DFF = 11008
NF = 86
FQ = [(0, 22), (22, 22), (44, 21), (65, 21)]
EPS = 1e-5
ENGINES = ("sync", "gpsimd", "tensor", "vector", "scalar")
ENG2SEM = {"sync": "sy", "gpsimd": "gq", "tensor": "tn",
           "vector": "vc", "scalar": "ac"}


class Builder:
    def __init__(self, nc, ctx):
        self.nc = nc
        self.ctx = ctx
        self.sched = {e: [] for e in ENGINES}
        self.sems = {}
        for name in ("sy", "gq", "tn", "vc", "ac"):
            self.new_sem(name)
        self.hiwater = {}

    def new_sem(self, name):
        h = self.ctx.enter_context(self.nc.semaphore(name))
        self.sems[name] = [h, 0]
        return name

    def emit(self, eng, fn):
        self.sched[eng].append(fn)

    def val(self, s):
        return self.sems[s][1]

    def wait(self, eng, s, v):
        if v is None or v <= 0:
            return
        if v <= self.hiwater.get((eng, s), 0):
            return
        self.hiwater[(eng, s)] = v
        h = self.sems[s][0]
        self.emit(eng, lambda q, h=h, v=v: q.wait_ge(h, v))

    def run(self, eng, fn, inc=None, n=1, fence=False):
        if fence and inc is None:
            inc = ENG2SEM[eng]
        if inc is None:
            self.emit(eng, fn)
            return None
        rec = self.sems[inc]
        rec[1] += n
        h = rec[0]
        self.emit(eng, lambda q, fn=fn, h=h, n=n: fn(q).then_inc(h, n))
        if fence:
            self.wait(eng, inc, rec[1])
        return rec[1]

    def qwait(self, eng, s):
        self.wait(eng, s, self.val(s))

    def dma(self, eng, out_ap, in_ap, sem=None, **kw):
        s = sem if sem else ("sy" if eng == "sync" else "gq")
        rec = self.sems[s]
        rec[1] += 16
        h = rec[0]
        self.emit(eng, lambda q, o=out_ap, i=in_ap, h=h, kw=kw:
                  q.dma_start(out=o, in_=i, **kw).then_inc(h, 16))
        return rec[1]

    def barrier(self):
        for eng, s in ENG2SEM.items():
            self.run(eng, lambda q: q.drain(), inc=s)
        snap = {s: self.val(s) for s in self.sems}
        for eng in ENGINES:
            for s, v in snap.items():
                if s != ENG2SEM[eng]:
                    self.wait(eng, s, v)

    def finalize(self):
        with self.nc.Block() as block:
            for eng in ENGINES:
                def body(q, eng=eng):
                    for f in self.sched[eng]:
                        f(q)
                getattr(block, eng)(body)


def build_nc(debug=False):
    nc = bass.Bass("TRN2", target_bir_lowering=False)

    x_ext = nc.dram_tensor("x_ext", [THL, D], F32, kind="ExternalInput").ap()
    conv_t = nc.dram_tensor("conv_t", [NWG, 3, ND, P, 256], F32, kind="ExternalInput").ap()
    conv_b24 = nc.dram_tensor("conv_b24", [NW3, P], F32, kind="ExternalInput").ap()
    wq2 = nc.dram_tensor("wq2", [WA, WA], F32, kind="ExternalInput").ap()
    wk2 = nc.dram_tensor("wk2", [WA, WA], F32, kind="ExternalInput").ap()
    wv2 = nc.dram_tensor("wv2", [WA, WA], F32, kind="ExternalInput").ap()
    wo2 = nc.dram_tensor("wo2", [WA, WA], F32, kind="ExternalInput").ap()
    bq8 = nc.dram_tensor("bq8", [8, P], F32, kind="ExternalInput").ap()
    bk8 = nc.dram_tensor("bk8", [8, P], F32, kind="ExternalInput").ap()
    bv_row = nc.dram_tensor("bv_row", [1, WA], F32, kind="ExternalInput").ap()
    bo8 = nc.dram_tensor("bo8", [8, P], F32, kind="ExternalInput").ap()
    projw = nc.dram_tensor("projw", [WA, D], F32, kind="ExternalInput").ap()
    pb_row = nc.dram_tensor("pb_row", [1, D], F32, kind="ExternalInput").ap()
    wg_t = nc.dram_tensor("wg_t", [NF, 8, P, 512], I32, kind="ExternalInput").ap()
    wu_t = nc.dram_tensor("wu_t", [NF, 8, P, 512], I32, kind="ExternalInput").ap()
    wd_t = nc.dram_tensor("wd_t", [NF, 8, P, 512], I32, kind="ExternalInput").ap()
    gs86 = nc.dram_tensor("gs86", [NF, P], F32, kind="ExternalInput").ap()
    us86 = nc.dram_tensor("us86", [NF, P], F32, kind="ExternalInput").ap()
    ds_row_d = nc.dram_tensor("ds_row", [1, D], F32, kind="ExternalInput").ap()
    out_d = nc.dram_tensor("out", [T, D], F32, kind="ExternalOutput").ap()
    spill_kind = "ExternalOutput" if debug else "Internal"
    spill = nc.dram_tensor("qkv_spill", [NW3, P, T], BF16, kind=spill_kind).ap()
    if debug:
        dbg_xnt = nc.dram_tensor("dbg_xnt", [ND, P, THL], BF16, kind="ExternalOutput").ap()
        dbg_h0 = nc.dram_tensor("dbg_h0", [22, P, T], BF16, kind="ExternalOutput").ap()
        dbg_heads = nc.dram_tensor("dbg_heads", [32, P, T], BF16, kind="ExternalOutput").ap()
        dbg_ow = nc.dram_tensor("dbg_ow", [8, P, T], BF16, kind="ExternalOutput").ap()
        dbg_att = nc.dram_tensor("dbg_att", [8, P, P], F32, kind="ExternalOutput").ap()
        dbg_q = nc.dram_tensor("dbg_q", [4, T, D], F32, kind="ExternalOutput").ap()

    with ExitStack() as ctx:
        b = Builder(nc, ctx)

        PS = ctx.enter_context(nc.psum_tensor("ps", [P, 4096], F32))
        PSB = PS.bitcast(BF16)

        def bankf(i, n=1):
            return PS[:, i * 512:(i + n) * 512]

        def bankb(i, n=1):
            return PSB[:, i * 1024:(i + n) * 1024]

        AR_BYTES = 212000
        arena = ctx.enter_context(nc.sbuf_tensor("arena", [P, AR_BYTES], U8))
        A32h = arena.bitcast(F32)
        ABFh = arena.bitcast(BF16)
        A32 = A32h.ap()
        ABF = ABFh.ap()

        class Alloc:
            cur = 0

            def take(self, nbytes, align=64):
                self.cur = (self.cur + align - 1) // align * align
                off = self.cur
                self.cur += nbytes
                assert self.cur <= AR_BYTES, f"SBUF arena overflow: {self.cur}"
                return off
        al = Alloc()

        def f32v(off, n, p0=0, pn=P):
            return A32[p0:p0 + pn, off // 4: off // 4 + n]

        def bf16v(off, n, p0=0, pn=P):
            return ABF[p0:p0 + pn, off // 2: off // 2 + n]

        o_identb = al.take(256)
        o_identf = al.take(512)
        o_mask = al.take(512)
        o_maskb = al.take(256)
        o_ones = al.take(512)
        o_rows = al.take(512)
        o_cb = al.take(128)
        o_sbq = al.take(64)
        o_bk = al.take(64)
        o_bo = al.take(64)
        o_gs = al.take(384)
        o_us = al.take(384)
        o_dsb = al.take(16384)
        o_pbrb = al.take(8192)
        o_onesb = al.take(256)
        o_bvb = al.take(4096)
        o_eps = al.take(64)
        o_bn = al.take(256)
        o_agg = al.take(64)
        o_lnt = al.take(64)
        o_rstd = al.take(64)
        o_negmb = al.take(64)
        REG_A = al.take(65664)
        REG_B = al.take(49152)
        REG_C = al.take(49152)
        assert al.cur <= AR_BYTES

        identb = bf16v(o_identb, 128)
        maskb = bf16v(o_maskb, 128)
        identf = f32v(o_identf, 128)
        mask = f32v(o_mask, 128)
        ones1 = f32v(o_ones, 128, 0, 1)
        ones1b = bf16v(o_onesb, 128, 0, 1)
        pbrow_b = bf16v(o_pbrb, D, 0, 1)

        def xnt(d_):
            return bf16v(REG_A + d_ * 2052, THL)

        def xnt3d(d0, nd_, c0, ncols):
            base = REG_A // 2
            return ABF[:, base:base + 32 * THL] \
                .rearrange("p (c n) -> p c n", n=THL)[:, d0:d0 + nd_, c0:c0 + ncols]

        def qkvT(w):
            return bf16v(REG_B + w * 2048, T)

        def hT(i):
            return bf16v(REG_B + i * 2048, T)

        def col(off, j):
            return A32[0:P, off // 4 + j: off // 4 + j + 1]

        def g_(fn, inc=None, n=1, fence=False):
            return b.run("gpsimd", fn, inc=inc, n=n, fence=fence)

        g_(lambda q: q.memset(identb, 0.0), inc="gq", fence=True)
        g_(lambda q: q.affine_select(out=identb, in_=identb, compare_op=ALU.not_equal,
                                     fill=1.0, base=0, pattern=[[-1, P]], channel_multiplier=1))
        g_(lambda q: q.memset(identf, 0.0), inc="gq", fence=True)
        g_(lambda q: q.affine_select(out=identf, in_=identf, compare_op=ALU.not_equal,
                                     fill=1.0, base=0, pattern=[[-1, P]], channel_multiplier=1))
        g_(lambda q: q.memset(mask, 0.0), inc="gq", fence=True)
        g_(lambda q: q.affine_select(out=mask, in_=mask, compare_op=ALU.is_ge,
                                     fill=-1e9, base=0, pattern=[[-1, P]], channel_multiplier=1))
        g_(lambda q: q.memset(maskb, 0.0), inc="gq", fence=True)
        g_(lambda q: q.affine_select(out=maskb, in_=maskb, compare_op=ALU.is_ge,
                                     fill=-1e9, base=0, pattern=[[-1, P]], channel_multiplier=1))
        g_(lambda q: q.memset(ones1, 1.0))
        g_(lambda q: q.memset(ones1b, 1.0))
        g_(lambda q: q.memset(f32v(o_eps, 1), EPS))
        setup_ms = g_(lambda q: q.drain(), inc="gq", n=1)

        rowbuf = lambda k: f32v(o_rows, 128, 0, k)

        def vec_to_cols(dram_ap, k, dest_off):
            t_ld = b.dma("sync", rowbuf(k), dram_ap)
            b.wait("tensor", "sy", t_ld)
            b.wait("tensor", "gq", setup_ms)
            ms = b.run("tensor", lambda q, k=k: q.transpose(
                PS[0:P, 0:k], rowbuf(k), identf[0:k, 0:k]), inc="tn")
            b.wait("vector", "tn", ms)
            b.run("vector", lambda q, k=k, d=dest_off: q.tensor_copy(
                A32[0:P, d // 4:d // 4 + k], PS[0:P, 0:k]), inc="vc")
            ev = b.run("vector", lambda q: q.drain(), inc="vc")
            b.wait("tensor", "vc", ev)
            b.wait("sync", "tn", ms)
            return ev

        vec_to_cols(conv_b24[:, :], NW3, o_cb)
        vec_to_cols(bq8[:, :], 8, o_sbq)
        vec_to_cols(bk8[:, :], 8, o_bk)
        vec_to_cols(bo8[:, :], 8, o_bo)
        vec_to_cols(gs86[:, :], NF, o_gs)
        ev_us = vec_to_cols(us86[:, :], NF, o_us)
        b.wait("scalar", "vc", ev_us)
        sbq_ms = b.run("scalar", lambda q: q.mul(
            A32[0:P, o_sbq // 4:o_sbq // 4 + 8],
            A32[0:P, o_sbq // 4:o_sbq // 4 + 8], 0.125), inc="ac")

        o_dsrow = REG_C
        o_pbrow = REG_C + 16384
        o_bvrow = REG_C + 32768
        b.new_sem("rw")
        b.new_sem("pbr")
        t1 = b.dma("sync", f32v(o_dsrow, D, 0, 1), ds_row_d, sem="rw")
        t3 = b.dma("sync", f32v(o_bvrow, WA, 0, 1), bv_row, sem="rw")
        pbr_t = b.dma("gpsimd", pbrow_b, pb_row, sem="pbr")
        b.wait("tensor", "rw", t3)
        for src_off, dst_off, n in ((o_dsrow, o_dsb, D),
                                    (o_bvrow, o_bvb, WA)):
            for j in range(n // 512):
                ms = b.run("tensor", lambda q, s=src_off, j=j: q.matmul(
                    bankf(1), ones1, A32[0:1, s // 4 + j * 512: s // 4 + (j + 1) * 512],
                    start=True, stop=True), inc="tn")
                b.wait("vector", "tn", ms)
                b.run("vector", lambda q, d=dst_off, j=j: q.tensor_copy(
                    A32[0:P, d // 4 + j * 512: d // 4 + (j + 1) * 512], bankf(1)), inc="vc")
                ev = b.run("vector", lambda q: q.drain(), inc="vc")
                b.wait("tensor", "vc", ev)

        b.barrier()

        o_xring = REG_B
        o_xnring = REG_B + 32768
        bn = f32v(o_bn, 48)
        agg = f32v(o_agg, 2)
        b.new_sem("xr0"); b.new_sem("xr1")
        xn_ms = {}
        trp_ms = {}
        ln_ld = {}
        ev_ms = {}
        for t_ in range(9):
            rt = 128 if t_ < 8 else 2
            xv = f32v(o_xring + (t_ % 2) * 16384, D, 0, rt)
            if t_ >= 2:
                b.wait("sync", "vc", xn_ms[t_ - 2])
            ln_ld[t_] = b.dma("sync", xv, x_ext[t_ * P: t_ * P + rt, :],
                              sem=f"xr{t_ % 2}")
            b.wait("vector", f"xr{t_ % 2}", ln_ld[t_])
            for i in range(8):
                b.run("vector", lambda q, xv=xv, i=i, rt=rt: q.bn_stats(
                    bn[0:rt, 6 * i:6 * i + 6], xv[0:rt, 512 * i:512 * (i + 1)]),
                    inc="vc" if i == 7 else None, fence=(i == 7))
            st_ms = b.run("vector", lambda q, rt=rt: q.bn_aggr(
                agg[0:rt, 0:2], bn[0:rt, 0:48]), inc="vc")
            b.wait("scalar", "vc", st_ms)
            b.run("scalar", lambda q, rt=rt: q.activation(
                f32v(o_lnt, 1, 0, rt), agg[0:rt, 1:2], AF.Ln,
                bias=f32v(o_eps, 1, 0, rt), scale=1.0), fence=True)
            rs_ms = b.run("scalar", lambda q, rt=rt: q.activation(
                f32v(o_rstd, 1, 0, rt), f32v(o_lnt, 1, 0, rt), AF.Exp,
                bias=0.0, scale=-0.5), inc="ac")
            b.wait("vector", "ac", rs_ms)
            nm_ms = b.run("vector", lambda q, rt=rt: q.scalar_tensor_tensor(
                f32v(o_negmb, 1, 0, rt), agg[0:rt, 0:1], -1.0,
                f32v(o_rstd, 1, 0, rt), op0=ALU.mult, op1=ALU.mult),
                inc="vc", fence=True)
            if t_ >= 2:
                b.wait("vector", "tn", trp_ms[t_ - 2])
            xnv = bf16v(o_xnring + (t_ % 2) * 8192, D, 0, rt)
            xn_ms[t_] = b.run("vector", lambda q, xv=xv, xnv=xnv, rt=rt: q.tensor_scalar(
                xnv, xv, f32v(o_rstd, 1, 0, rt), f32v(o_negmb, 1, 0, rt),
                op0=ALU.mult, op1=ALU.add), inc="vc", fence=True)
            b.wait("tensor", "vc", xn_ms[t_])
            for g in range(8):
                gg = t_ * 8 + g
                bk_ = 2 + gg % 2
                if gg >= 2:
                    b.wait("tensor", "vc", ev_ms[gg - 2])
                for j in range(4):
                    d_ = g * 4 + j
                    last = (j == 3)
                    m = b.run("tensor", lambda q, bk_=bk_, j=j, d_=d_, xnv=xnv, rt=rt: q.transpose(
                        bankb(bk_)[:, j * 128:j * 128 + rt],
                        xnv[0:rt, d_ * 128:(d_ + 1) * 128], identb[0:rt, 0:rt]),
                        inc="tn" if last else None)
                    if last:
                        grp_ms = m
                if g == 7:
                    trp_ms[t_] = grp_ms
                b.wait("vector", "tn", grp_ms)
                src3 = bankb(bk_).rearrange("p (c n) -> p c n", n=128)[:, 0:4, 0:rt]
                ev_ms[gg] = b.run("vector", lambda q, g=g, t_=t_, rt=rt, src3=src3:
                                  q.tensor_copy(xnt3d(g * 4, 4, t_ * 128, rt), src3),
                                  inc="vc")
        b.barrier()
        if debug:
            for d_ in range(ND):
                b.dma("sync", dbg_xnt[d_, :, :], xnt(d_))
            b.barrier()

        o_cring = REG_C
        CR = 12
        for s_ in range(CR):
            b.new_sem(f"cr{s_}")
        cons = []
        evac_ms = {}
        for wg in range(NWG):
            base = 0 if wg % 2 == 0 else 4
            if wg >= 2:
                b.wait("tensor", "ac", evac_ms[wg - 2])
            for tap in range(3):
                for d_ in range(ND):
                    gi = wg * 96 + tap * 32 + d_
                    slot = gi % CR
                    if gi >= CR:
                        b.wait("gpsimd", "tn", cons[gi - CR])
                    ld = b.dma("gpsimd", bf16v(o_cring + slot * 512, 256),
                               conv_t[wg, tap, d_, :, :], sem=f"cr{slot}")
                    b.wait("tensor", f"cr{slot}", ld)
                    first = (tap == 0 and d_ == 0)
                    lastk = (tap == 2 and d_ == ND - 1)
                    for wt in range(2):
                        for c in range(2):
                            bk_ = base + wt * 2 + c
                            is_last_of_tile = (wt == 1 and c == 1)
                            b.run("tensor", lambda q, bk_=bk_, slot=slot, wt=wt, c=c,
                                  d_=d_, tap=tap, first=first, lastk=lastk: q.matmul(
                                bankf(bk_),
                                bf16v(o_cring + slot * 512, 256)[:, wt * 128:(wt + 1) * 128],
                                xnt(d_)[:, c * 512 + tap: c * 512 + tap + 512],
                                start=first, stop=lastk),
                                inc="tn" if is_last_of_tile else None)
                    cons.append(b.val("tn"))
            stop_all = cons[-1]
            b.wait("scalar", "tn", stop_all)
            for wt in range(2):
                for c in range(2):
                    w_ = wg * 2 + wt
                    e = b.run("scalar", lambda q, base=base, wt=wt, c=c, w_=w_: q.activation(
                        qkvT(w_)[:, c * 512:(c + 1) * 512], bankf(base + wt * 2 + c),
                        AF.Identity, bias=col(o_cb, w_), scale=1.0), inc="ac")
            evac_ms[wg] = e
            e = b.run("scalar", lambda q: q.drain(), inc="ac")
            b.wait("sync", "ac", e)
            for wt in range(2):
                w_ = wg * 2 + wt
                b.dma("sync", spill[w_, :, :], qkvT(w_))
        b.barrier()

        o_sil = REG_C
        o_gring = REG_C + 8192
        o_uring = REG_C + 19456
        o_wdring = REG_C + 30720
        o_ftmp = REG_C + 36864
        o_pin = REG_C + 40960
        GR = 11
        WR = 6
        for s_ in range(GR):
            b.new_sem(f"gr{s_}")
            b.new_sem(f"ur{s_}")
        for s_ in range(WR):
            b.new_sem(f"wr{s_}")
        gslotv = lambda s: bf16v(o_gring + s * 1024, 512)
        uslotv = lambda s: bf16v(o_uring + s * 1024, 512)
        wdslotv = lambda s: bf16v(o_wdring + s * 1024, 512)
        ftmpv = lambda s: f32v(o_ftmp + s * 2048, 512)
        pinv = lambda s: f32v(o_pin + s * 2048, 512)
        eslotv = lambda s: f32v(o_gring + s * 2048, 512)
        eslot_rel = {}

        for s_ in range(4):
            b.new_sem(f"pn{s_}")
        gu_cons = []
        gu_loads = []
        flush_ms = {}
        for qf, (f0, nf_) in enumerate(FQ):
            sil_ms = {}
            h_ms = {}
            for fi in range(nf_):
                ft = f0 + fi
                par = fi % 2
                pgate = PS[:, (par * 2048):(par * 2048 + 1024)]
                pup = PS[:, (par * 2048 + 1024):(par * 2048 + 2048)]
                if fi >= 2:
                    b.wait("tensor", "ac", sil_ms[fi - 2])
                    b.wait("tensor", "vc", h_ms[fi - 2])
                ldg = [None] * 8
                ldu = [None] * 8
                for dg in range(8):
                    gi = len(gu_loads)
                    if gi >= 2 * GR:
                        b.wait("gpsimd", "tn", gu_cons[gi - 2 * GR])
                    sl = (fi * 8 + dg) % GR
                    ldg[dg] = (sl, b.dma("gpsimd", gslotv(sl), wg_t[ft, dg, :, :],
                                          sem=f"gr{sl}"))
                    gu_loads.append(ldg[dg]); gu_cons.append(None)
                    gi = len(gu_loads)
                    if gi >= 2 * GR:
                        b.wait("gpsimd", "tn", gu_cons[gi - 2 * GR])
                    ldu[dg] = (sl, b.dma("gpsimd", uslotv(sl), wu_t[ft, dg, :, :],
                                          sem=f"ur{sl}"))
                    gu_loads.append(ldu[dg]); gu_cons.append(None)
                for d_ in range(ND):
                    dg, j = d_ // 4, d_ % 4
                    sl = (fi * 8 + dg) % GR
                    b.wait("tensor", f"gr{sl}", ldg[dg][1])
                    b.wait("tensor", f"ur{sl}", ldu[dg][1])
                    first = (d_ == 0)
                    lastk = (d_ == ND - 1)
                    for mat, psum, ring in ((0, pgate, gslotv(sl)), (1, pup, uslotv(sl))):
                        for c in range(2):
                            do_inc = (mat == 1 and c == 1 and j == 3)
                            b.run("tensor", lambda q, psum=psum, ring=ring, j=j, c=c,
                                  d_=d_, first=first, lastk=lastk: q.matmul(
                                psum[:, c * 512:(c + 1) * 512],
                                ring[:, j * 128:(j + 1) * 128],
                                xnt(d_)[:, 1 + c * 512: 1 + c * 512 + 512],
                                start=first, stop=lastk),
                                inc="tn" if do_inc else None)
                    if j == 3:
                        v = b.val("tn")
                        gu_cons[len(gu_loads) - 16 + dg * 2] = v
                        gu_cons[len(gu_loads) - 16 + dg * 2 + 1] = v
                stop_ms = b.val("tn")
                b.wait("scalar", "tn", stop_ms)
                if fi >= 2:
                    b.wait("scalar", "vc", h_ms[fi - 2])
                sil_ms[fi] = b.run("scalar", lambda q, par=par, pgate=pgate, ft=ft: q.activation(
                    f32v(o_sil + par * 4096, 1024), pgate, AF.Silu,
                    bias=0.0, scale=col(o_gs, ft)), inc="ac")
                b.wait("vector", "ac", sil_ms[fi])
                b.wait("vector", "tn", stop_ms)
                h_ms[fi] = b.run("vector", lambda q, fi=fi, par=par, pup=pup, ft=ft: q.scalar_tensor_tensor(
                    hT(fi), pup, col(o_us, ft), f32v(o_sil + par * 4096, 1024),
                    op0=ALU.mult, op1=ALU.mult), inc="vc")
            last_sil = sil_ms[nf_ - 1]
            last_h = h_ms[nf_ - 1]
            wd_cons = []
            wd_loads = []
            b.wait("tensor", "ac", last_sil)
            b.wait("tensor", "vc", last_h)
            for dt in range(8):
                bank_stop = [None] * 8
                pin_src = (lambda ts, dt: x_ext[1 + ts * P: 1 + (ts + 1) * P,
                                                 dt * 512:(dt + 1) * 512]) if qf == 0 \
                    else (lambda ts, dt: out_d[ts * P:(ts + 1) * P,
                                               dt * 512:(dt + 1) * 512])
                pin_ld = {ts: b.dma("sync", pinv(ts % 4), pin_src(ts, dt),
                                    sem=f"pn{ts % 4}")
                    for ts in range(4)}
                for fi in range(nf_):
                    ft = f0 + fi
                    gi = len(wd_loads)
                    if gi >= WR:
                        b.wait("gpsimd", "tn", wd_cons[gi - WR])
                    sl = gi % WR
                    ld = b.dma("gpsimd", wdslotv(sl), wd_t[ft, dt, :, :],
                               sem=f"wr{sl}")
                    wd_loads.append(ld)
                    b.wait("tensor", f"wr{sl}", ld)
                    for ts in range(8):
                        if fi == 0 and dt > 0:
                            b.wait("tensor", *flush_ms[(dt - 1, ts)])
                        do_inc = (ts == 7) or (fi == nf_ - 1)
                        m = b.run("tensor", lambda q, ts=ts, fi=fi, sl=sl, nf_=nf_: q.matmul(
                            bankf(ts), hT(fi)[:, ts * 128:(ts + 1) * 128], wdslotv(sl),
                            start=(fi == 0), stop=(fi == nf_ - 1)),
                            inc="tn" if do_inc else None)
                        if fi == nf_ - 1:
                            bank_stop[ts] = b.val("tn")
                    wd_cons.append(b.val("tn"))
                for ts in range(8):
                    eng = "scalar" if ts % 2 else "vector"
                    sem = "ac" if ts % 2 else "vc"
                    b.wait(eng, "tn", bank_stop[ts])
                    b.wait(eng, "vc", eslot_rel.get(ts, 0))
                    m = b.run(eng, lambda q, ts=ts, eng=eng:
                              (q.copy if eng == "scalar" else q.tensor_copy)(
                                  eslotv(ts), bankf(ts)), inc=sem, fence=True)
                    flush_ms[(dt, ts)] = (sem, m)
                for ts in range(8):
                    sl = ts % 4
                    if ts >= 4:
                        pin_ld[ts] = b.dma("sync", pinv(sl), pin_src(ts, dt),
                                           sem=f"pn{sl}")
                    sem, m = flush_ms[(dt, ts)]
                    b.wait("vector", sem, m)
                    b.wait("vector", f"pn{sl}", pin_ld[ts])
                    eslot_rel[ts] = b.run("vector", lambda q, ts=ts, dt=dt: q.tensor_tensor(
                        ftmpv(ts % 2), eslotv(ts),
                        A32[0:P, o_dsb // 4 + dt * 512: o_dsb // 4 + (dt + 1) * 512],
                        op=ALU.mult), inc="vc", fence=True)
                    fm = b.run("vector", lambda q, ts=ts, sl=sl: q.tensor_tensor(
                        pinv(sl), ftmpv(ts % 2), pinv(sl), op=ALU.add), inc="vc")
                    b.wait("sync", "vc", fm)
                    b.dma("sync", out_d[ts * P:(ts + 1) * P, dt * 512:(dt + 1) * 512],
                          pinv(sl))
                    if debug:
                        b.dma("sync", dbg_q[qf, ts * P:(ts + 1) * P,
                                            dt * 512:(dt + 1) * 512], pinv(sl))
            if debug and qf == 0:
                b.wait("sync", "vc", last_h)
                for fi in range(nf_):
                    b.dma("sync", dbg_h0[fi, :, :], hT(fi))
            b.barrier()
            flush_ms = {}

        qhT = lambda ht: bf16v(REG_A + ht * 2048, T)
        khT = lambda ht: bf16v(REG_A + 16384 + ht * 2048, T)
        vh = lambda ts: bf16v(REG_A + 32768 + ts * 2048, T)
        oT = lambda ht: bf16v(REG_A + 49152 + ht * 2048, T)
        owT = lambda wt: bf16v(REG_C + wt * 2048, T)
        hw = lambda wc: bf16v(REG_C + 16384 + wc * 2048, T)
        o_pring = REG_C + 32768
        o_pin3 = REG_C + 35840
        o_sm = REG_C + 41984
        o_pf = o_sm + 2048
        o_pbf = o_pf + 2048
        o_ptb = o_pbf + 1024
        o_nm = o_ptb + 512
        o_ssum = o_nm + 128
        o_rcp = o_ssum + 128
        pring = lambda s: bf16v(o_pring + s * 1024, 512)
        pin3 = lambda s: f32v(o_pin3 + s * 2048, 512)
        for s_ in range(3):
            b.new_sem(f"pr{s_}")
            b.new_sem(f"pq{s_}")
        for s_ in range(NWA):
            b.new_sem(f"hw{s_}")
        b.new_sem("qk")

        for w in range(NW3):
            b.dma("sync", qkvT(w), spill[w, :, :], sem="qk")
        qk_all = b.val("qk")

        hb_prev = [None, None]
        last_mm = [0]

        def head_proj(wsrc, kind):
            b.wait("gpsimd", "tn", last_mm[0])
            hld = [b.dma("gpsimd", hw(wc), wsrc[wc * P:(wc + 1) * P, :],
                         sem=f"hw{wc}") for wc in range(NWA)]
            ev = {}
            if kind == "v":
                for ts in range(8):
                    par = ts % 2
                    ps = PS[:, par * 2048: par * 2048 + 1024]
                    if ts >= 2:
                        b.wait("tensor", "vc", ev[ts - 2])
                    elif hb_prev[par] is not None:
                        b.wait("tensor", *hb_prev[par])
                    for wc in range(NWA):
                        b.wait("tensor", f"hw{wc}", hld[wc])
                        b.wait("tensor", "qk", qk_all)
                        for c in range(2):
                            do_inc = (wc == NWA - 1 and c == 1)
                            b.run("tensor", lambda q, ps=ps, wc=wc, c=c, ts=ts: q.matmul(
                                ps[:, c * 512:(c + 1) * 512],
                                qkvT(16 + wc)[:, ts * 128:(ts + 1) * 128],
                                hw(wc)[:, c * 512:(c + 1) * 512],
                                start=(wc == 0), stop=(wc == NWA - 1)),
                                inc="tn" if do_inc else None)
                    sm_ = b.val("tn")
                    b.wait("vector", "tn", sm_)
                    ev[ts] = b.run("vector", lambda q, ts=ts, ps=ps: q.tensor_tensor(
                        vh(ts), ps, A32[0:P, o_bvb // 4: o_bvb // 4 + 1024],
                        op=ALU.add), inc="vc")
                hb_prev[0] = ("vc", ev[6])
                hb_prev[1] = ("vc", ev[7])
                last_mm[0] = b.val("tn")
                return ev[7]
            else:
                src0 = 0 if kind == "q" else 8
                dest = qhT if kind == "q" else khT
                bias_off = o_sbq if kind == "q" else o_bk
                scale = 0.125 if kind == "q" else 1.0
                for ht in range(NWA):
                    par = ht % 2
                    ps = PS[:, par * 2048: par * 2048 + 1024]
                    if ht >= 2:
                        b.wait("tensor", "ac", ev[ht - 2])
                    elif hb_prev[par] is not None:
                        b.wait("tensor", *hb_prev[par])
                    for wc in range(NWA):
                        b.wait("tensor", f"hw{wc}", hld[wc])
                        b.wait("tensor", "qk", qk_all)
                        for c in range(2):
                            do_inc = (wc == NWA - 1 and c == 1)
                            b.run("tensor", lambda q, ps=ps, wc=wc, c=c, ht=ht, src0=src0: q.matmul(
                                ps[:, c * 512:(c + 1) * 512],
                                hw(wc)[:, ht * 128:(ht + 1) * 128],
                                qkvT(src0 + wc)[:, c * 512:(c + 1) * 512],
                                start=(wc == 0), stop=(wc == NWA - 1)),
                                inc="tn" if do_inc else None)
                    sm_ = b.val("tn")
                    b.wait("scalar", "tn", sm_)
                    ev[ht] = b.run("scalar", lambda q, ht=ht, ps=ps, dest=dest,
                                   bias_off=bias_off, scale=scale: q.activation(
                        dest(ht), ps, AF.Identity, bias=col(bias_off, ht), scale=scale),
                        inc="ac")
                hb_prev[0] = ("ac", ev[6])
                hb_prev[1] = ("ac", ev[7])
                last_mm[0] = b.val("tn")
                return ev[7]

        head_proj(wq2, "q")
        head_proj(wk2, "k")
        head_proj(wv2, "v")
        b.barrier()
        if debug:
            for j in range(8):
                b.dma("sync", dbg_heads[j, :, :], qhT(j))
                b.dma("sync", dbg_heads[8 + j, :, :], khT(j))
                b.dma("sync", dbg_heads[16 + j, :, :], vh(j))
            b.barrier()

        sm_ms = {}
        exp_ms = {}
        rcp_ms = {}
        pbf_ms = {}
        pt_ms = {}
        ptev_ms = {}
        pv_ms = {}
        oev_ms = {}
        sum_ms4 = {}
        bankrel = {}
        for blk in range(8):
            for h in range(NH):
                i = blk * NH + h
                par = i % 4
                ip = i // 2
                par2 = ip % 2
                ht = h // 2
                hrow = (h % 2) * 64
                ps_s = PS[:, par * 512: par * 512 + 128]
                if i >= 4:
                    b.wait("tensor", bankrel[i - 4][0], bankrel[i - 4][1])
                b.run("tensor", lambda q, ps_s=ps_s: q.matmul(
                    ps_s, identb, maskb, start=True, stop=False))
                b.run("tensor", lambda q, ps_s=ps_s, ht=ht, hrow=hrow, blk=blk: q.matmul(
                    ps_s,
                    qhT(ht)[hrow:hrow + 64, blk * 128:(blk + 1) * 128],
                    khT(ht)[hrow:hrow + 64, blk * 128:(blk + 1) * 128],
                    start=False, stop=True), inc="tn")
                s_ms = b.val("tn")
                b.wait("vector", "tn", s_ms)
                if i >= 4:
                    b.wait("vector", "ac", exp_ms[i - 4])
                sm_ms[i] = b.run("vector", lambda q, ps_s=ps_s, par=par: q.reduce_max(
                    f32v(o_nm + par * 32, 1), ps_s, axis=AX.X, negate=True), inc="vc")
                b.wait("scalar", "vc", sm_ms[i])
                if i >= 4:
                    b.wait("scalar", "vc", pbf_ms[i - 4])
                pfv = f32v(o_pf + par * 512, 128)
                exp_ms[i] = b.run("scalar", lambda q, pfv=pfv, ps_s=ps_s, par=par: q.activation(
                    pfv, ps_s, AF.Exp, bias=f32v(o_nm + par * 32, 1), scale=1.0), inc="ac")
                bankrel[i] = ("ac", exp_ms[i])
                b.wait("vector", "ac", exp_ms[i])
                if i >= 4:
                    b.wait("vector", "ac", rcp_ms[i - 4])
                sum_ms4[i] = b.run("vector", lambda q, pfv=pfv, par=par: q.reduce_sum(
                    f32v(o_ssum + par * 32, 1), pfv, axis=AX.X), inc="vc")
                b.wait("scalar", "vc", sum_ms4[i])
                b.run("scalar", lambda q, par=par: q.activation(
                    f32v(o_rcp + par * 32, 1), f32v(o_ssum + par * 32, 1),
                    AF.Ln, bias=0.0, scale=1.0), fence=True)
                rcp_ms[i] = b.run("scalar", lambda q, par=par: q.activation(
                    f32v(o_rcp + par * 32, 1), f32v(o_rcp + par * 32, 1),
                    AF.Exp, bias=0.0, scale=-1.0), inc="ac", fence=True)
                b.wait("vector", "ac", rcp_ms[i])
                pbfv = bf16v(o_pbf + par * 256, 128)
                if i >= 4:
                    b.wait("vector", "tn", pt_ms[i - 4])
                pbf_ms[i] = b.run("vector", lambda q, pbfv=pbfv, pfv=pfv, par=par: q.tensor_scalar(
                    pbfv, pfv, f32v(o_rcp + par * 32, 1), None, op0=ALU.mult), inc="vc")
                b.wait("tensor", "vc", pbf_ms[i])
                if i >= 2:
                    b.wait("tensor", "vc", ptev_ms[i - 2])
                b.run("tensor", lambda q, i=i, pbfv=pbfv: q.transpose(
                    bankb(4 + i % 2)[:, 0:128], pbfv, identb), inc="tn")
                pt_ms[i] = b.val("tn")
                b.wait("vector", "tn", pt_ms[i])
                ptbv = bf16v(o_ptb + (i % 2) * 256, 128)
                if i >= 2:
                    b.wait("vector", "tn", pv_ms[i - 2])
                ptev_ms[i] = b.run("vector", lambda q, ptbv=ptbv, i=i: q.tensor_copy(
                    ptbv, bankb(4 + i % 2)[:, 0:128]), inc="vc")
                if debug and i < 2:
                    dsl = i * 4
                    b.wait("vector", "ac", exp_ms[i])
                    c1 = b.run("vector", lambda q, dsl=dsl, ps_s=ps_s: q.tensor_copy(
                        f32v(o_sil + 2048 * (dsl % 2), 128), ps_s), inc="vc")
                    bankrel[i] = ("vc", c1)
                    b.wait("sync", "vc", c1)
                    b.dma("sync", dbg_att[dsl, :, :], f32v(o_sil + 2048 * (dsl % 2), 128))
                    b.dma("sync", dbg_att[dsl + 1, :, :], f32v(o_sil + 2048 * (dsl % 2), 128))
                    b.dma("sync", dbg_att[dsl + 2, :, :], pfv)
                    b.wait("sync", "tn", pt_ms[i])
                    c2 = b.run("vector", lambda q, dsl=dsl, i=i: q.tensor_copy(
                        f32v(o_sil + 2048 * (dsl % 2), 128), bankb(4 + i % 2)[:, 0:128]), inc="vc")
                    ptev_ms[i] = c2
                    b.wait("sync", "vc", c2)
                    b.dma("sync", dbg_att[dsl + 3, :, :], f32v(o_sil + 2048 * (dsl % 2), 128))
                b.wait("tensor", "vc", ptev_ms[i])
                if h % 2 == 0 and ip >= 2:
                    b.wait("tensor", "vc", oev_ms[ip - 2])
                b.run("tensor", lambda q, hrow=hrow, par2=par2, blk=blk, h=h, ptbv=ptbv: q.matmul(
                    PS[hrow:hrow + 64, (6 + par2) * 512: (6 + par2) * 512 + 128],
                    vh(blk)[:, h * 64:(h + 1) * 64], ptbv,
                    start=True, stop=True), inc="tn")
                pv_ms[i] = b.val("tn")
                if h % 2 == 1:
                    op_ms = pv_ms[i]
                    b.wait("vector", "tn", op_ms)
                    oev_ms[ip] = b.run("vector", lambda q, ht=ht, blk=blk, par2=par2: q.tensor_copy(
                        oT(ht)[:, blk * 128:(blk + 1) * 128],
                        PS[:, (6 + par2) * 512: (6 + par2) * 512 + 128]), inc="vc")
        b.barrier()
        if debug:
            for j in range(8):
                b.dma("sync", dbg_heads[24 + j, :, :], oT(j))
            b.barrier()

        hld = [b.dma("gpsimd", hw(wc), wo2[wc * P:(wc + 1) * P, :],
                     sem=f"hw{wc}") for wc in range(NWA)]
        evo = {}
        for wt in range(NWA):
            par = wt % 2
            ps = PS[:, par * 2048: par * 2048 + 1024]
            if wt >= 2:
                b.wait("tensor", "ac", evo[wt - 2])
            for wc in range(NWA):
                b.wait("tensor", f"hw{wc}", hld[wc])
                for c in range(2):
                    do_inc = (wc == NWA - 1 and c == 1)
                    b.run("tensor", lambda q, ps=ps, wc=wc, c=c, wt=wt: q.matmul(
                        ps[:, c * 512:(c + 1) * 512],
                        hw(wc)[:, wt * 128:(wt + 1) * 128],
                        oT(wc)[:, c * 512:(c + 1) * 512],
                        start=(wc == 0), stop=(wc == NWA - 1)),
                        inc="tn" if do_inc else None)
            sm_ = b.val("tn")
            b.wait("scalar", "tn", sm_)
            evo[wt] = b.run("scalar", lambda q, wt=wt, ps=ps: q.activation(
                owT(wt), ps, AF.Identity, bias=col(o_bo, wt), scale=1.0), inc="ac")
        b.barrier()
        if debug:
            for j in range(8):
                b.dma("sync", dbg_ow[j, :, :], owT(j))
            b.barrier()

        pw_cons = []
        pw_loads = []
        fin_ms = {}
        ptmp_st = {}
        for dt in range(8):
            bank_stop = [None] * 8
            pin_ld = {ts: b.dma("sync", pin3(ts % 3), out_d[
                ts * P:(ts + 1) * P, dt * 512:(dt + 1) * 512],
                sem=f"pq{ts % 3}")
                for ts in range(3)}
            for wc in range(NWA):
                gi = len(pw_loads)
                if gi >= 3:
                    b.wait("gpsimd", "tn", pw_cons[gi - 3])
                sl = gi % 3
                ld = b.dma("gpsimd", pring(sl), projw[wc * P:(wc + 1) * P, dt * 512:(dt + 1) * 512],
                           sem=f"pr{sl}")
                pw_loads.append(ld)
                b.wait("tensor", f"pr{sl}", ld)
                for ts in range(8):
                    if wc == 0:
                        if dt > 0:
                            b.wait("tensor", "vc", fin_ms[(dt - 1, ts)])
                        b.wait("tensor", "pbr", pbr_t)
                        b.run("tensor", lambda q, ts=ts, dt=dt: q.matmul(
                            bankf(ts), ones1b,
                            pbrow_b[0:1, dt * 512:(dt + 1) * 512],
                            start=True, stop=False))
                    do_inc = (ts == 7) or (wc == NWA - 1)
                    b.run("tensor", lambda q, ts=ts, wc=wc, sl=sl: q.matmul(
                        bankf(ts), owT(wc)[:, ts * 128:(ts + 1) * 128], pring(sl),
                        start=False, stop=(wc == NWA - 1)),
                        inc="tn" if do_inc else None)
                    if wc == NWA - 1:
                        bank_stop[ts] = b.val("tn")
                pw_cons.append(b.val("tn"))
            for ts in range(8):
                sl = ts % 3
                if ts >= 3:
                    pin_ld[ts] = b.dma("sync", pin3(sl), out_d[
                        ts * P:(ts + 1) * P, dt * 512:(dt + 1) * 512],
                        sem=f"pq{sl}")
                b.wait("vector", "tn", bank_stop[ts])
                b.wait("vector", f"pq{sl}", pin_ld[ts])
                fin_ms[(dt, ts)] = b.run("vector", lambda q, ts=ts, sl=sl: q.tensor_tensor(
                    pin3(sl), bankf(ts), pin3(sl), op=ALU.add), inc="vc")
                b.wait("sync", "vc", fin_ms[(dt, ts)])
                b.dma("sync", out_d[ts * P:(ts + 1) * P, dt * 512:(dt + 1) * 512],
                      pin3(sl))
        b.wait("sync", "gq", b.val("gq"))
        b.wait("sync", "sy", b.val("sy"))

        b.finalize()
    return nc


def _prep_shared(inputs):
    a = {k: np.asarray(v) for k, v in inputs.items()}
    conv_w = a["L0_attn_conv_w"].astype(np.float32, copy=False)
    conv_t = np.ascontiguousarray(
        conv_w.reshape(3, ND, P, NWG, 256).transpose(3, 0, 1, 2, 4))
    gw = a["L0_ffn_gate_w"]
    uw = a["L0_ffn_up_w"]
    dw = a["L0_ffn_down_w"]
    wgT = np.ascontiguousarray(gw.T)
    wuT = np.ascontiguousarray(uw.T)
    wdT = np.ascontiguousarray(dw.T)
    wg_t = np.ascontiguousarray(
        wgT.reshape(ND, P, NF, P).transpose(2, 0, 1, 3)
           .reshape(NF, 8, 4, P, P).transpose(0, 1, 3, 2, 4).reshape(NF, 8, P, 512))
    wu_t = np.ascontiguousarray(
        wuT.reshape(ND, P, NF, P).transpose(2, 0, 1, 3)
           .reshape(NF, 8, 4, P, P).transpose(0, 1, 3, 2, 4).reshape(NF, 8, P, 512))
    wd_t = np.ascontiguousarray(
        wdT.reshape(NF, P, 8, 512).transpose(0, 2, 1, 3))
    shared = {
        "conv_t": conv_t,
        "conv_b24": np.ascontiguousarray(a["L0_attn_conv_b"].reshape(NW3, P).astype(np.float32, copy=False)),
        "wq2": np.ascontiguousarray(a["L0_attn_wq"].reshape(WA, WA).astype(np.float32, copy=False)),
        "wk2": np.ascontiguousarray(a["L0_attn_wk"].reshape(WA, WA).astype(np.float32, copy=False)),
        "wv2": np.ascontiguousarray(a["L0_attn_wv"].reshape(WA, WA).astype(np.float32, copy=False)),
        "wo2": np.ascontiguousarray(a["L0_attn_wo"].reshape(WA, WA).astype(np.float32, copy=False)),
        "bq8": np.ascontiguousarray(a["L0_attn_bq"].reshape(8, P).astype(np.float32, copy=False)),
        "bk8": np.ascontiguousarray(a["L0_attn_bk"].reshape(8, P).astype(np.float32, copy=False)),
        "bv_row": np.ascontiguousarray(a["L0_attn_bv"].reshape(1, WA).astype(np.float32, copy=False)),
        "bo8": np.ascontiguousarray(a["L0_attn_bo"].reshape(8, P).astype(np.float32, copy=False)),
        "projw": np.ascontiguousarray(a["L0_attn_proj_w"].astype(np.float32, copy=False)),
        "pb_row": np.ascontiguousarray(a["L0_attn_proj_b"].reshape(1, D).astype(np.float32, copy=False)),
        "wg_t": wg_t, "wu_t": wu_t, "wd_t": wd_t,
        "gs86": np.ascontiguousarray(a["L0_ffn_gate_s"].reshape(NF, P).astype(np.float32, copy=False)),
        "us86": np.ascontiguousarray(a["L0_ffn_up_s"].reshape(NF, P).astype(np.float32, copy=False)),
        "ds_row": np.ascontiguousarray(a["L0_ffn_down_s"].reshape(1, D).astype(np.float32, copy=False)),
    }
    return shared, a


def _shard_x(x, c):
    n, h = c // 2, c % 2
    xe = np.zeros((THL, D), np.float32)
    if h == 0:
        xe[1:THL] = x[n, 0:THL - 1]
    else:
        xe[0:THL - 1] = x[n, T - 1:2 * T]
    return xe


def kernel(**inputs):
    shared, a = _prep_shared(inputs)
    x = a["L0_attn_inputs"].astype(np.float32, copy=False)
    nc = build_nc()
    in_maps = []
    for c in range(8):
        m = dict(shared)
        m["x_ext"] = _shard_x(x, c)
        in_maps.append(m)
    res = run_bass_kernel_spmd(nc, in_maps, core_ids=list(range(8)))
    out = np.empty((4, 2048, D), np.float32)
    for c, r in enumerate(res.results):
        n, h = c // 2, c % 2
        out[n, h * T:(h + 1) * T] = r["out"]
    return out
